# version 06
# baseline (speedup 1.0000x reference)
import os
import numpy as np
import ml_dtypes
import concourse.bass as bass
import concourse.mybir as mybir
from concourse.bass_utils import run_bass_kernel_spmd

F32, BF16, I32 = mybir.dt.float32, mybir.dt.bfloat16, mybir.dt.int32
AF, ALU = mybir.ActivationFunctionType, mybir.AluOpType
BF = ml_dtypes.bfloat16
TOK, D, KVH, G, DH = 1024, 4096, 8, 4, 128
SCALE = DH ** -0.5
LN10 = float(np.log(1e-10))


KTRUNC = int(os.environ.get("KTRUNC", "99"))


def build_nc():
    nc = bass.Bass("TRN2", target_bir_lowering=False)
    dr, sb = {}, {}

    def din(n, shp, dt):
        dr[n] = nc.dram_tensor(n, list(shp), dt, kind="ExternalInput").ap()

    def dint(n, shp, dt):
        dr[n] = nc.dram_tensor(n, list(shp), dt, kind="Internal").ap()

    def asb(n, shp, dt):
        sb[n] = nc.alloc_sbuf_tensor(n, list(shp), dt).ap()

    din("x_in", (128, 32, TOK), F32)
    din("qv", (128, 4, 4096), I32)
    din("qs", (128, 4, 128), F32)
    din("wqkv_s", (768, D), F32)
    din("wo_s", (512, D), F32)
    din("kw1_s", (256, 2048), F32)
    din("vw1_s", (256, 2048), F32)
    din("wg", (128, D), F32)
    din("kw2", (128, 2048), F32)
    din("vw2", (128, 2048), F32)
    din("ones1f", (1, 128), F32)
    din("identf", (128, 128), F32)
    for n, shp in (("kb1", (128, 16)), ("vb1", (128, 16)), ("kb2", (128, 1)),
                   ("vb2", (128, 1)), ("bgate", (128, 1)), ("ffnb", (128, 32)),
                   ("kpos", (128, 128)), ("vpos", (128, 128)), ("memk", (128, 8)),
                   ("memv", (1, 1024)), ("cosF", (128, TOK)), ("sinF", (128, TOK))):
        din(n, shp, F32)
    for n, shp in (("ident", (128, 128)), ("swp", (128, 128)), ("stat2", (128, 4)),
                   ("ones1", (1, 128)), ("zo", (128, 256)), ("zo1", (1, 256)),
                   ("b3v", (3, 384)), ("emask", (128, 2048)), ("cmask", (128, TOK)),
                   ("ownm", (128, 256)), ("slm", (128, 2048))):
        din(n, shp, BF16)
    out_hbm = nc.dram_tensor("out", [128, 32, TOK], F32, kind="ExternalOutput").ap()

    dint("wc_f", (4, 128, D), BF16)
    dint("wc_r", (12, 128, D), BF16)
    dint("wg_f", (8, 512, D), BF16)
    dint("wg_r", (8, 1536, D), BF16)
    KVE = 128 * 8192 + 128 * 8192 + 128 * 512 + 64 * 1024
    dint("kc_in", (KVE,), BF16)
    dint("kc_out", (2, KVE), BF16)
    for n in ("x0h", "x1h", "x2h", "lnh", "qrh", "qph", "ath"):
        dint(n, (128, 32, TOK), BF16)
    dint("wgl", (128, 4096), BF16)
    dint("kw2l", (128, 2048), BF16)
    dint("vw2l", (128, 2048), BF16)

    asb("mv", (128, 2, 512), BF16)
    asb("wst", (128, 2, 512), BF16)
    asb("AR1", (128, 32768), BF16)
    asb("xsq", (128, 512), BF16)
    asb("mu", (1, 512), F32)
    asb("rr", (1, 512), F32)
    asb("mubc", (128, 512), F32)
    asb("rbc", (128, 512), F32)
    asb("t32", (128, 2, 512), F32)
    asb("X1", (128, 8192), BF16)
    asb("X2", (128, 8192), BF16)
    asb("X3", (128, 8192), BF16)
    asb("X4", (128, 8192), BF16)
    asb("ckf", (128, 8, 64), BF16)
    asb("cvf", (128, 8, 128), BF16)
    asb("vtmp", (128, 512), BF16)
    asb("g_all", (128, TOK), BF16)
    asb("ck_sb", (128, 8, 2, 64), BF16)
    asb("cv_sb", (128, 8, 128), BF16)
    asb("qg", (128, 512), BF16)
    asb("rqg", (128, 512), BF16)
    asb("pexpg", (128, 4, 512), BF16)
    asb("pexpm", (1, 4, 512), BF16)
    asb("m_sb", (128, 512), F32)
    asb("sel01", (128, 512), BF16)
    asb("selneg", (128, 512), BF16)
    asb("mx8", (128, 32), F32)
    asb("sm", (128, 8), F32)
    asb("cB", (128, 2), F32)
    asb("tq", (128, 256), F32)
    asb("zs", (128, 512), F32)
    asb("sclk", (3, 512), BF16)
    asb("scb", (128, 512), BF16)
    asb("pex", (128, 512), BF16)
    asb("ao", (128, 512), BF16)
    asb("ost", (128, 1, 512), F32)
    asb("memkb", (128, 8), BF16)
    asb("memvb", (1, 1024), BF16)
    CON_F32 = ("kb1", "vb1", "kb2", "vb2", "bgate", "ffnb", "kpos", "vpos",
               "ones1f", "identf")
    CON_BF = ("ident", "swp", "stat2", "ones1", "zo", "zo1", "b3v", "emask",
              "cmask", "ownm", "slm")
    for n in CON_F32:
        asb("c_" + n, dr[n].shape, F32)
    for n in CON_BF:
        asb("c_" + n, dr[n].shape, BF16)
    asb("c_cosF", (128, TOK), BF16)
    asb("c_sinF", (128, TOK), BF16)
    asb("ffnbh", (128, 32), F32)
    pb = [nc.alloc_psum_tensor(f"pb{i}", [128, 512], F32).ap() for i in range(8)]
    sb["st32"] = sb["AR1"][:, 0:16384].bitcast(F32).rearrange("p (a b) -> p a b", a=2)
    sb["stbf"] = sb["AR1"][:, 16384:24576].rearrange("p (a b) -> p a b", a=2)
    sb["rk_sb"] = sb["AR1"][:, 0:16384].rearrange("p (a b c) -> p a b c", a=2, b=8)
    sb["v_sb"] = sb["AR1"][:, 16384:32768].rearrange("p (a b c d) -> p a b c d", a=2, b=8, c=8)
    sb["k_sb"] = sb["X1"].rearrange("p (a b) -> p a b", a=8)
    sb["hid"] = sb["X1"].rearrange("p (a b) -> p a b", a=16)
    sb["rk_own"] = sb["X2"].rearrange("p (a b) -> p a b", a=8)
    sb["vpv"] = sb["X2"].rearrange("p (a b c) -> p a b c", a=8, b=64)
    sb["vf"] = sb["X3"].rearrange("p (a b) -> p a b", a=8)
    sb["kpk"] = sb["X4"].rearrange("p (a b c) -> p a b c", a=8, b=64)
    sb["qs_sb"] = sb["wst"][:, :, :].rearrange("p a b -> p (a b)").bitcast(F32)
    sb["g1"] = sb["t32"][:, 0, :]
    sb["g2"] = sb["t32"][:, 1, :]
    sb["scv"] = sb["ost"][:, 0, :]
    pbf = [p.bitcast(BF16) for p in pb]

    ops = {e: [] for e in ("tensor", "vector", "scalar", "gpsimd", "sync")}
    cnt = {k: 0 for k in ("t", "v", "s", "dS", "dG", "dA", "cc")}

    def W(eng, sem, val=None):
        v = cnt[sem] if val is None else val
        if v > 0:
            ops[eng].append(("w", sem, v))

    def mk(eng, sem, by=1):
        def add(fn):
            cnt[sem] += by
            ops[eng].append(("o", fn, sem, by))
        return add

    T = mk("tensor", "t")
    _V, _Sc = mk("vector", "v"), mk("scalar", "s")

    def V(fn):
        W("vector", "v", cnt["v"])
        _V(fn)

    def Sc(fn):
        W("scalar", "s", cnt["s"])
        _Sc(fn)
    _DS, _DG, _DA = mk("sync", "dS", 16), mk("gpsimd", "dG", 16), mk("scalar", "dA", 16)

    def DS(fn):
        W("sync", "dS", cnt["dS"])
        _DS(fn)

    def DG(fn):
        W("gpsimd", "dG", cnt["dG"])
        _DG(fn)

    def DA(fn):
        W("scalar", "dA", cnt["dA"])
        _DA(fn)
    CC = mk("gpsimd", "cc", 1)

    for n in CON_F32 + CON_BF:
        DS(lambda e, n=n: e.dma_start(out=sb["c_" + n][tuple(slice(None) for _ in sb["c_" + n].shape)],
                                      in_=dr[n][tuple(slice(None) for _ in dr[n].shape)]))
    DG(lambda e: e.dma_start(out=sb["memvb"][:], in_=dr["memv"][:, :]))
    DG(lambda e: e.dma_start(out=sb["memkb"][:], in_=dr["memk"][:, :]))
    DG(lambda e: e.dma_start(out=sb["c_cosF"][:], in_=dr["cosF"][:, :]))
    DG(lambda e: e.dma_start(out=sb["c_sinF"][:], in_=dr["sinF"][:, :]))
    W("vector", "dS")
    W("vector", "dG")
    V(lambda e: e.tensor_scalar_mul(sb["ffnbh"][:], sb["c_ffnb"][:], 0.5))
    V(lambda e: e.memset(sb["cB"][:, 0:1], -32.0))
    V(lambda e: e.memset(sb["cB"][:, 1:2], -1000.0))

    W("gpsimd", "dG")
    for i in range(8):
        W("gpsimd", "v", cnt["v"])
        DG(lambda e, i=i: e.dma_start(out=sb["st32"][:, i % 2, :],
                                      in_=dr["x_in"][:, 4 * i:4 * i + 4, :].rearrange("p a b -> p (a b)")))
        W("vector", "dG")
        W("vector", "dA", cnt["dA"])
        V(lambda e, i=i: e.tensor_copy(
            out=sb["stbf"][:, i % 2, :],
            in_=sb["st32"][:, i % 2, :]))
        W("scalar", "v")
        DA(lambda e, i=i: e.dma_start(out=dr["x0h"][:, 4 * i:4 * i + 4, :].rearrange("p a b -> p (a b)"),
                                      in_=sb["stbf"][:, i % 2, :]))

    stbff = sb["stbf"].rearrange("p a b -> p (a b)")
    st32v = sb["st32"]
    for t in range(4):
        W("sync", "s", cnt["s"])
        W("sync", "v", cnt["v"])
        DS(lambda e, t=t: e.dma_start(out=st32v[:, t % 2, :].bitcast(I32), in_=dr["qv"][:, t, :]))
        DS(lambda e, t=t: e.dma_start(out=sb["qs_sb"][:, 128 * (t % 2):128 * (t % 2) + 128], in_=dr["qs"][:, t, :]))
        W("vector", "dS")
        W("vector", "dA", cnt["dA"])
        V(lambda e, t=t: e.scalar_tensor_tensor(
            out=stbff[:, 4096 * (t % 2):4096 * (t % 2) + 4096].rearrange("p (b j) -> p b j", j=32),
            in0=st32v[:, t % 2, :].bitcast(I32).rearrange("p (b j) -> p b j", j=32),
            scalar=128.0,
            in1=sb["qs_sb"][:, 128 * (t % 2):128 * (t % 2) + 128].unsqueeze(2).to_broadcast([128, 128, 32]),
            op0=ALU.subtract, op1=ALU.mult))
        W("scalar", "v")
        DA(lambda e, t=t: e.dma_start(out=dr["wc_f"][t, :, :],
                                      in_=stbff[:, 4096 * (t % 2):4096 * (t % 2) + 4096]))
    RS = [("wqkv_s", 0, 6, 4096), ("wo_s", 6, 4, 4096), ("kw1_s", 10, 1, 2048), ("vw1_s", 11, 1, 2048)]
    for src, base, nt, cols in RS:
        for t in range(nt):
            W("sync", "s", cnt["s"])
            W("sync", "v", cnt["v"])
            if cols == 4096:
                DS(lambda e, src=src, t=t: e.dma_start(out=st32v[:, t % 2, :],
                                                       in_=dr[src][128 * t:128 * t + 128, :]))
            else:
                DS(lambda e, src=src, t=t: e.dma_start(out=st32v[:, t % 2, 0:2048],
                                                       in_=dr[src][128 * (2 * t):128 * (2 * t) + 128, :]))
                DS(lambda e, src=src, t=t: e.dma_start(out=st32v[:, t % 2, 2048:4096],
                                                       in_=dr[src][128 * (2 * t + 1):128 * (2 * t + 1) + 128, :]))
            W("vector", "dS")
            W("vector", "dA", cnt["dA"])
            V(lambda e, t=t: e.tensor_copy(out=stbff[:, 4096 * (t % 2):4096 * (t % 2) + 4096],
                                           in_=st32v[:, t % 2, :]))
            W("scalar", "v")
            DA(lambda e, base=base, t=t: e.dma_start(out=dr["wc_r"][base + t, :, :],
                                                     in_=stbff[:, 4096 * (t % 2):4096 * (t % 2) + 4096]))

    for srcd, dstd in (("kw2", "kw2l"), ("vw2", "vw2l"), ("wg", "wgl")):
        ncols = dr[srcd].shape[1]
        W("sync", "dA")
        W("sync", "v", cnt["v"])
        DS(lambda e, srcd=srcd, ncols=ncols: e.dma_start(
            out=sb["st32"][:, 0, 0:ncols], in_=dr[srcd][:, 0:ncols]))
        W("vector", "dS")
        W("vector", "dA", cnt["dA"])
        V(lambda e, ncols=ncols: e.tensor_copy(out=sb["stbf"][:, 0, 0:ncols],
                                               in_=sb["st32"][:, 0, 0:ncols]))
        W("scalar", "v")
        DA(lambda e, dstd=dstd, ncols=ncols: e.dma_start(out=dr[dstd][:, 0:ncols],
                                                         in_=sb["stbf"][:, 0, 0:ncols]))
    W("gpsimd", "dA")
    W("gpsimd", "dS")
    CC(lambda e: e.collective_compute("AllGather", ALU.bypass,
                                      replica_groups=[[0, 1, 2, 3, 4, 5, 6, 7]],
                                      ins=[dr["wc_f"][:, :, :].rearrange("a p f -> (a p f)")],
                                      outs=[dr["wg_f"][:, :, :].rearrange("a p f -> (a p f)")]))
    CC(lambda e: e.collective_compute("AllGather", ALU.bypass,
                                      replica_groups=[[0, 1, 2, 3, 4, 5, 6, 7]],
                                      ins=[dr["wc_r"][:, :, :].rearrange("a p f -> (a p f)")],
                                      outs=[dr["wg_r"][:, :, :].rearrange("a p f -> (a p f)")]))

    def layernorm(src, dst):
        W("sync", "dA")
        for th in range(2):
            for ic in range(32):
                buf = ic % 2
                W("sync", "t", cnt["t"])
                W("sync", "s", cnt["s"])
                W("sync", "dA", cnt["dA"])
                DS(lambda e, ic=ic, th=th, buf=buf: e.dma_start(
                    out=sb["mv"][:, buf, :], in_=dr[src][:, ic, 512 * th:512 * th + 512]))
                W("scalar", "dS")
                W("scalar", "v", cnt["v"])
                Sc(lambda e, buf=buf: e.activation(out=sb["xsq"][:, :], in_=sb["mv"][:, buf, :],
                                                   func=AF.Square, bias=0.0, scale=1.0))
                W("tensor", "s")
                W("tensor", "dS")
                T(lambda e, ic=ic, th=th, buf=buf: e.matmul(pb[6][0:1, :], sb["c_stat2"][:, 0:1],
                                                            sb["mv"][:, buf, :], start=(ic == 0), stop=(ic == 31)))
                T(lambda e, ic=ic: e.matmul(pb[5][0:1, :], sb["c_stat2"][:, 0:1], sb["xsq"][:, :],
                                            start=(ic == 0), stop=(ic == 31)))
            W("vector", "t")
            V(lambda e: e.tensor_scalar_mul(sb["mu"][:, :], pb[6][0:1, :], 1.0 / 4096))
            V(lambda e: e.tensor_scalar_mul(sb["zs"][0:1, 0:512], pb[5][0:1, :], 1.0 / 4096))
            V(lambda e: e.tensor_tensor(out=sb["g1"][0:1, :], in0=sb["mu"][:, :],
                                        in1=sb["mu"][:, :], op=ALU.mult))
            V(lambda e: e.tensor_tensor(out=sb["zs"][0:1, 0:512], in0=sb["zs"][0:1, 0:512],
                                        in1=sb["g1"][0:1, :], op=ALU.subtract))
            V(lambda e: e.tensor_scalar_add(sb["zs"][0:1, 0:512], sb["zs"][0:1, 0:512], 1e-5))
            W("scalar", "v")
            Sc(lambda e: e.activation(out=sb["zs"][0:1, 0:512], in_=sb["zs"][0:1, 0:512],
                                      func=AF.Sqrt, bias=0.0, scale=1.0))
            W("vector", "s")
            V(lambda e: e.reciprocal(out=sb["zs"][0:1, 0:512], in_=sb["zs"][0:1, 0:512]))
            V(lambda e: e.tensor_copy(out=sb["rr"][:, :], in_=sb["zs"][0:1, 0:512]))
            W("tensor", "v")
            T(lambda e: e.matmul(pb[6][0:128, :], sb["c_ones1f"][:, :],
                                 sb["mu"][0:1, :], start=True, stop=True))
            T(lambda e: e.matmul(pb[7][0:128, :], sb["c_ones1f"][:, :],
                                 sb["rr"][0:1, :], start=True, stop=True))
            W("vector", "t")
            V(lambda e: e.tensor_copy(out=sb["mubc"][:, :], in_=pb[6][0:128, :]))
            V(lambda e: e.tensor_copy(out=sb["rbc"][:, :], in_=pb[7][0:128, :]))
            for ic in range(32):
                buf = ic % 2
                W("sync", "v", cnt["v"])
                W("sync", "dA", cnt["dA"])
                DS(lambda e, ic=ic, th=th, buf=buf: e.dma_start(
                    out=sb["mv"][:, buf, :], in_=dr[src][:, ic, 512 * th:512 * th + 512]))
                W("vector", "dS")
                V(lambda e, buf=buf: e.tensor_tensor(out=sb["t32"][:, 0, :], in0=sb["mv"][:, buf, :],
                                                     in1=sb["mubc"][:, :], op=ALU.subtract))
                V(lambda e, buf=buf: e.tensor_tensor(out=sb["mv"][:, buf, :], in0=sb["t32"][:, 0, :],
                                                     in1=sb["rbc"][:, :], op=ALU.mult))
                W("scalar", "v")
                DA(lambda e, ic=ic, th=th, buf=buf: e.dma_start(
                    out=dr[dst][:, ic, 512 * th:512 * th + 512], in_=sb["mv"][:, buf, :]))

    def wsrc_pieces(kind, og, ic):
        if kind == "ffn":
            return [(dr["wg_f"][og, :, 128 * ic:128 * ic + 128], 0, 512)]
        if kind == "qkv":
            r0, out = 512 * og, []
            col = 0
            while col < 512:
                r = r0 + col
                c, lr = divmod(r, 768)
                n = min(512 - col, 768 - lr)
                out.append((dr["wg_r"][c, lr:lr + n, 128 * ic:128 * ic + 128], col, n))
                col += n
            return out
        if kind == "wo":
            return [(dr["wg_r"][og, 768:1280, 128 * ic:128 * ic + 128], 0, 512)]
        if kind in ("kw1", "vw1"):
            r0 = 1280 if kind == "kw1" else 1408
            out = []
            for j, c2 in enumerate((2 * og, 2 * og + 1)):
                for h_ in range(2):
                    out.append((dr["wg_r"][c2, r0:r0 + 128,
                                           2048 * h_ + 128 * ic:2048 * h_ + 128 * ic + 128],
                                256 * j + 128 * h_, 128))
            return out

    def stream_mm(kind, nog, nic, moving, evac, nth=2):
        for og in range(nog):
            for ic in range(nic):
                buf = ic % 2
                W("sync", "t", cnt["t"])
                for ap_, col0, ncols in wsrc_pieces(kind, og, ic):
                    DS(lambda e, ap_=ap_, col0=col0, buf=buf: e.dma_start(
                        out=sb["wst"][:, buf, col0:col0 + ap_.shape[0]], in_=ap_, transpose=True))
                W("tensor", "dS")
                for ol in range(4):
                    for th in range(nth):
                        bank = pb[2 * ol + th]
                        T(lambda e, ic=ic, ol=ol, th=th, buf=buf, bank=bank, nic=nic: e.matmul(
                            bank[0:128, :], sb["wst"][:, buf, 128 * ol:128 * ol + 128],
                            moving(ic, th), start=(ic == 0), stop=(ic == nic - 1)))
            for ol in range(4):
                for th in range(nth):
                    evac(4 * og + ol, th, pb[2 * ol + th])

    def mload(src):
        def f(ic, th):
            return sb["mv"][:, (2 * ic + th) % 2, :]
        def prefetch(ic, th):
            buf = (2 * ic + th) % 2
            W("sync", "t", cnt["t"])
            DS(lambda e, ic=ic, th=th, buf=buf: e.dma_start(
                out=sb["mv"][:, buf, :], in_=dr[src][:, ic, 512 * th:512 * th + 512]))
            W("tensor", "dS")
        return f, prefetch

    def stream_mm_dram(kind, nog, nic, src, evac, nth=2):
        W("sync", "dA")
        W("sync", "cc")
        for og in range(nog):
            W("tensor", "v", cnt["v"])
            W("tensor", "s", cnt["s"])
            for ic in range(nic):
                for th in range(nth):
                    buf = th
                    W("sync", "t", cnt["t"])
                    DS(lambda e, ic=ic, th=th, buf=buf, src=src: e.dma_start(
                        out=sb["mv"][:, buf, :], in_=dr[src][:, ic, 512 * th:512 * th + 512]))
                bufw = nic and (ic % 2)
                for ap_, col0, ncols in wsrc_pieces(kind, og, ic):
                    DS(lambda e, ap_=ap_, col0=col0, bufw=bufw: e.dma_start(
                        out=sb["wst"][:, bufw, col0:col0 + ap_.shape[0]], in_=ap_, transpose=True))
                W("tensor", "dS")
                for ol in range(4):
                    for th in range(nth):
                        T(lambda e, ic=ic, ol=ol, th=th, bufw=bufw, nic=nic: e.matmul(
                            pb[2 * ol + th][0:128, :], sb["wst"][:, bufw, 128 * ol:128 * ol + 128],
                            sb["mv"][:, th, :], start=(ic == 0), stop=(ic == nic - 1)))
            for ol in (3, 0, 1, 2):
                for th in range(nth):
                    evac(4 * og + ol, th, pb[2 * ol + th])

    if KTRUNC < 1:
        return finish(nc, ops, cnt, W, DS, sb, out_hbm, block_emit)
    W("sync", "cc", 1)
    W("sync", "dA")
    layernorm("x0h", "lnh")

    def ffn_evac(xsrc, xdst, final=False):
        def evac(oc, th, bank):
            W("scalar", "v", cnt["v"])
            DA(lambda e, oc=oc, th=th: e.dma_start(out=sb["xsq"][:, :],
                                                   in_=dr[xsrc][:, oc, 512 * th:512 * th + 512]))
            W("vector", "t")
            W("vector", "dA")
            V(lambda e, oc=oc, bank=bank: e.tensor_scalar(
                out=sb["t32"][:, 0, :], in0=bank[0:128, :], scalar1=0.5,
                scalar2=sb["ffnbh"][:, oc:oc + 1], op0=ALU.mult, op1=ALU.add))
            if final:
                V(lambda e, oc=oc, th=th: e.tensor_tensor(
                    out=sb["ost"][:, 0, :], in0=sb["t32"][:, 0, :],
                    in1=sb["xsq"][:, :], op=ALU.add))
                W("scalar", "v")
                DA(lambda e, oc=oc, th=th: e.dma_start(
                    out=out_hbm[:, oc, 512 * th:512 * th + 512], in_=sb["ost"][:, 0, :]))
            else:
                V(lambda e, oc=oc, th=th: e.tensor_tensor(
                    out=sb["xsq"][:, :], in0=sb["t32"][:, 0, :],
                    in1=sb["xsq"][:, :], op=ALU.add))
                W("scalar", "v")
                DA(lambda e, oc=oc, th=th: e.dma_start(
                    out=dr[xdst][:, oc, 512 * th:512 * th + 512], in_=sb["xsq"][:, :]))
        return evac

    if KTRUNC >= 2:
        stream_mm_dram("ffn", 8, 32, "lnh", ffn_evac("x0h", "x1h"))
    if KTRUNC < 3:
        return finish(nc, ops, cnt, W, DS, sb, out_hbm, block_emit)

    layernorm("x1h", "lnh")
    if KTRUNC < 4:
        return finish(nc, ops, cnt, W, DS, sb, out_hbm, block_emit)

    def qkv_evac(oc, th, bank):
        if oc < 32:
            W("vector", "t")
            W("vector", "dA", cnt["dA"])
            V(lambda e, bank=bank: e.tensor_copy(out=sb["qg"][:, :], in_=bank[0:128, :]))
            W("scalar", "v")
            DA(lambda e, oc=oc, th=th: e.dma_start(out=dr["qrh"][:, oc, 512 * th:512 * th + 512],
                                                   in_=sb["qg"][:, :]))
            W("tensor", "v")
            T(lambda e: e.matmul(pb[6][0:128, :], sb["c_swp"][:, :], sb["qg"][:, :], start=True, stop=True))
            W("vector", "t")
            V(lambda e, th=th: e.tensor_tensor(out=sb["t32"][:, 0, :], in0=sb["qg"][:, :],
                                               in1=sb["c_cosF"][:, 512 * th:512 * th + 512], op=ALU.mult))
            V(lambda e, th=th: e.tensor_tensor(out=sb["t32"][:, 1, :], in0=pb[6][0:128, :],
                                               in1=sb["c_sinF"][:, 512 * th:512 * th + 512], op=ALU.mult))
            W("vector", "dA", cnt["dA"])
            V(lambda e: e.tensor_tensor(out=sb["qg"][:, :], in0=sb["t32"][:, 0, :],
                                        in1=sb["t32"][:, 1, :], op=ALU.add))
            W("scalar", "v")
            DA(lambda e, oc=oc, th=th: e.dma_start(out=dr["qph"][:, oc, 512 * th:512 * th + 512],
                                                   in_=sb["qg"][:, :]))
        elif oc < 40:
            kvh = oc - 32
            W("vector", "t")
            V(lambda e, kvh=kvh, th=th, bank=bank: e.tensor_copy(
                out=sb["k_sb"][:, kvh, 512 * th:512 * th + 512], in_=bank[0:128, :]))
            W("tensor", "v")
            T(lambda e, kvh=kvh, th=th: e.matmul(pb[6][0:128, :], sb["c_swp"][:, :],
                                                 sb["k_sb"][:, kvh, 512 * th:512 * th + 512], start=True, stop=True))
            W("vector", "t")
            V(lambda e, kvh=kvh, th=th: e.tensor_tensor(
                out=sb["t32"][:, 0, :], in0=sb["k_sb"][:, kvh, 512 * th:512 * th + 512],
                in1=sb["c_cosF"][:, 512 * th:512 * th + 512], op=ALU.mult))
            V(lambda e, th=th: e.tensor_tensor(out=sb["t32"][:, 1, :], in0=pb[6][0:128, :],
                                               in1=sb["c_sinF"][:, 512 * th:512 * th + 512], op=ALU.mult))
            V(lambda e, kvh=kvh, th=th: e.tensor_tensor(
                out=sb["rk_own"][:, kvh, 512 * th:512 * th + 512], in0=sb["t32"][:, 0, :],
                in1=sb["t32"][:, 1, :], op=ALU.add))
        else:
            kvh = oc - 40
            W("vector", "t")
            V(lambda e, kvh=kvh, th=th, bank=bank: e.tensor_copy(
                out=sb["vf"][:, kvh, 512 * th:512 * th + 512], in_=bank[0:128, :]))

    stream_mm_dram("qkv", 12, 32, "lnh", qkv_evac)

    if KTRUNC < 5:
        return finish(nc, ops, cnt, W, DS, sb, out_hbm, block_emit)
    W("sync", "dA")
    W("tensor", "v", cnt["v"])
    W("tensor", "s", cnt["s"])
    for ic in range(32):
        for th in range(2):
            W("sync", "t", cnt["t"])
            DS(lambda e, ic=ic, th=th: e.dma_start(out=sb["mv"][:, th, :],
                                                   in_=dr["lnh"][:, ic, 512 * th:512 * th + 512]))
        DS(lambda e, ic=ic: e.dma_start(out=sb["wst"][:, ic % 2, 0:128],
                                        in_=dr["wgl"][:, 128 * ic:128 * ic + 128], transpose=True))
        W("tensor", "dS")
        for th in range(2):
            T(lambda e, ic=ic, th=th: e.matmul(pb[6 + th][0:128, :], sb["wst"][:, ic % 2, 0:128],
                                               sb["mv"][:, th, :], start=(ic == 0), stop=(ic == 31)))
    for th in range(2):
        W("scalar", "t")
        Sc(lambda e, th=th: e.activation(out=sb["g_all"][:, 512 * th:512 * th + 512], in_=pb[6 + th][0:128, :],
                                         func=AF.Sigmoid, bias=sb["c_bgate"][:, 0:1], scale=1.0))

    if KTRUNC < 6:
        return finish(nc, ops, cnt, W, DS, sb, out_hbm, block_emit)
    W("vector", "s")
    V(lambda e: e.tensor_tensor(out=sb["kpk"][:, :, :, :],
                                in0=sb["k_sb"][:, :, :].rearrange("p h (b j) -> p h b j", j=16),
                                in1=sb["c_kpos"][:, :].rearrange("p (h j) -> p h j", j=16).unsqueeze(2).to_broadcast([128, 8, 64, 16]),
                                op=ALU.add))
    V(lambda e: e.tensor_tensor(out=sb["vpv"][:, :, :, :],
                                in0=sb["vf"][:, :, :].rearrange("p h (b j) -> p h b j", j=16),
                                in1=sb["c_vpos"][:, :].rearrange("p (h j) -> p h j", j=16).unsqueeze(2).to_broadcast([128, 8, 64, 16]),
                                op=ALU.add))

    def mlp(w1kind, inb, bias1, w2T, bias2, dst):
        for og in range(4):
            W("tensor", "v", cnt["v"])
            W("tensor", "s", cnt["s"])
            for ci in range(16):
                buf = ci % 2
                W("sync", "t", cnt["t"])
                for ap_, col0, ncols in wsrc_pieces(w1kind, og, ci):
                    DS(lambda e, ap_=ap_, col0=col0, buf=buf: e.dma_start(
                        out=sb["wst"][:, buf, col0:col0 + ap_.shape[0]], in_=ap_, transpose=True))
                W("tensor", "dS")
                for ol in range(4):
                    T(lambda e, ci=ci, ol=ol, buf=buf: e.matmul(
                        pb[ol][0:128, :], sb["wst"][:, buf, 128 * ol:128 * ol + 128],
                        sb[inb][:, :, :, ci].rearrange("p h b -> p (h b)"), start=(ci == 0), stop=(ci == 15)))
            for ol in range(4):
                hc = 4 * og + ol
                W("scalar", "t")
                Sc(lambda e, hc=hc, ol=ol, bias1=bias1: e.activation(
                    out=sb["hid"][:, hc, :], in_=pb[ol][0:128, :], func=AF.Relu,
                    bias=sb[bias1][:, hc:hc + 1], scale=1.0))
        W("tensor", "s")
        for hc in range(16):
            W("sync", "t", cnt["t"])
            DS(lambda e, hc=hc, w2T=w2T: e.dma_start(out=sb["wst"][:, hc % 2, 0:128],
                                                     in_=dr[w2T][:, 128 * hc:128 * hc + 128], transpose=True))
            W("tensor", "dS")
            T(lambda e, hc=hc: e.matmul(pb[4][0:128, :], sb["wst"][:, hc % 2, 0:128],
                                        sb["hid"][:, hc, :], start=(hc == 0), stop=(hc == 15)))
        W("vector", "t")
        dstap = (sb[dst][:, :, :].rearrange("p a b -> p (a b)") if len(sb[dst].shape) == 3
                 else sb[dst][:, :])
        V(lambda e, bias2=bias2, dstap=dstap: e.tensor_scalar(
            out=dstap, in0=pb[4][0:128, :],
            scalar1=sb[bias2][:, 0:1], scalar2=None, op0=ALU.add))

    mlp("kw1", "kpk", "c_kb1", "kw2l", "c_kb2", "ckf")
    mlp("vw1", "vpv", "c_vb1", "vw2l", "c_vb2", "pex")
    pexv = sb["pex"].rearrange("p (h b) -> p h b", b=64)
    for kvh in range(8):
        W("tensor", "v")
        T(lambda e, kvh=kvh: e.transpose(pbf[5][0:64, 0:128], pexv[:, kvh, :], sb["c_ident"][:, :]))
        W("vector", "t")
        V(lambda e, kvh=kvh: e.tensor_copy(out=sb["cvf"][0:64, kvh, :], in_=pbf[5][0:64, 0:128]))

    for kvh in range(8):
        for tc in range(8):
            bank = tc % 2
            W("tensor", "v")
            T(lambda e, kvh=kvh, tc=tc, bank=bank: e.transpose(
                pbf[bank][0:128, 0:128], sb["vf"][:, kvh, 128 * tc:128 * tc + 128], sb["c_ident"][:, :]))
            W("vector", "t")
            W("vector", "dG", cnt["dG"])
            V(lambda e, bank=bank: e.tensor_copy(out=sb["vtmp"][:, 0:128], in_=pbf[bank][0:128, 0:128]))
            W("gpsimd", "v")
            DG(lambda e, kvh=kvh, tc=tc: e.dma_start(
                out=dr["kc_in"][128 * 8192:2 * 128 * 8192].rearrange("(p c) -> p c", p=128)[:, 1024 * tc + 128 * kvh:1024 * tc + 128 * kvh + 128],
                in_=sb["vtmp"][:, 0:128]))
    W("gpsimd", "v")
    DG(lambda e: e.dma_start(out=dr["kc_in"][0:128 * 8192].rearrange("(p c) -> p c", p=128),
                             in_=sb["rk_own"][:, :, :].rearrange("p a b -> p (a b)")))
    CKB = 2 * 128 * 8192
    DG(lambda e: e.dma_start(
        out=dr["kc_in"][CKB:CKB + 128 * 512].rearrange("(p c) -> p c", p=128),
        in_=sb["ckf"][:, :, :].rearrange("p a b -> p (a b)")))
    DG(lambda e: e.dma_start(
        out=dr["kc_in"][CKB + 128 * 512:CKB + 128 * 512 + 64 * 1024].rearrange("(p c) -> p c", p=64),
        in_=sb["cvf"][0:64, :, :].rearrange("p a b -> p (a b)")))
    W("gpsimd", "dG")
    CC(lambda e: e.collective_compute("AllGather", ALU.bypass,
                                      replica_groups=[[0, 1], [2, 3], [4, 5], [6, 7]],
                                      ins=[dr["kc_in"][:]], outs=[dr["kc_out"][:, :].rearrange("a b -> (a b)")]))
    W("sync", "cc")
    for s_ in range(2):
        DS(lambda e, s_=s_: e.dma_start(out=sb["rk_sb"][:, s_, :, :].rearrange("p a b -> p (a b)"),
                                        in_=dr["kc_out"][s_, 0:128 * 8192].rearrange("(p c) -> p c", p=128)))
        DS(lambda e, s_=s_: e.dma_start(
            out=sb["v_sb"][:, s_, :, :, :].rearrange("p a b c -> p (a b c)"),
            in_=dr["kc_out"][s_, 128 * 8192:2 * 128 * 8192].rearrange("(p c) -> p c", p=128)))
        DS(lambda e, s_=s_: e.dma_start(out=sb["ck_sb"][:, :, s_, :],
                                        in_=dr["kc_out"][s_, CKB:CKB + 128 * 512].rearrange("(p h b) -> p h b", p=128, h=8)))
        DS(lambda e, s_=s_: e.dma_start(
            out=sb["cv_sb"][64 * s_:64 * s_ + 64, :, :].rearrange("p a b -> p (a b)"),
            in_=dr["kc_out"][s_, CKB + 128 * 512:CKB + 128 * 512 + 64 * 1024].rearrange("(p c) -> p c", p=64)))

    if KTRUNC < 7:
        return finish(nc, ops, cnt, W, DS, sb, out_hbm, block_emit)
    ZOC, ZO1C = sb["c_zo"], sb["c_zo1"]
    W("sync", "dA")
    for half in range(2):
        W("vector", "t", cnt["t"])
        W("vector", "dS")
        V(lambda e: e.memset(pb[4][0:128, :], 0.0))
        for kvh in range(8):
            for g in range(4):
                W("sync", "t", cnt["t"])
                DS(lambda e, kvh=kvh, g=g, half=half: e.dma_start(
                    out=sb["qg"][:, :], in_=dr["qrh"][:, 4 * kvh + g, 512 * half:512 * half + 512]))
                W("tensor", "dS")
                W("tensor", "v", cnt["v"])
                W("tensor", "s", cnt["s"])
                T(lambda e, kvh=kvh: e.matmul(pb[0][0:128, :],
                                              sb["ck_sb"][:, kvh, :, :].rearrange("p a b -> p (a b)"),
                                              sb["qg"][:, :], start=True, stop=False))
                T(lambda e, half=half: e.matmul(pb[0][0:128, :], sb["c_ident"][:, :],
                                                sb["c_cmask"][:, 512 * half:512 * half + 512],
                                                start=False, stop=True))
                T(lambda e, kvh=kvh: e.matmul(pb[5][0:1, :], sb["memkb"][:, kvh:kvh + 1],
                                              sb["qg"][:, :], start=True, stop=True))
                W("vector", "t")
                if g == 0:
                    V(lambda e: e.tensor_scalar_mul(sb["m_sb"][:, :], pb[0][0:128, :], 1.0))
                else:
                    V(lambda e: e.tensor_tensor(out=sb["m_sb"][:, :], in0=sb["m_sb"][:, :],
                                                in1=pb[0][0:128, :], op=ALU.add))
                W("scalar", "v")
                W("scalar", "t", cnt["t"])
                Sc(lambda e, g=g: e.activation(out=sb["pexpg"][:, g, :], in_=pb[0][0:128, :],
                                               func=AF.Exp, bias=sb["cB"][:, 0:1], scale=SCALE))
                Sc(lambda e, g=g: e.activation(out=sb["pexpm"][:, g, :], in_=pb[5][0:1, :],
                                               func=AF.Exp, bias=sb["cB"][0:1, 0:1], scale=SCALE))
            W("tensor", "v")
            for qc in range(4):
                T(lambda e, qc=qc: e.transpose(pb[6][0:128, 128 * qc:128 * qc + 128],
                                               sb["m_sb"][:, 128 * qc:128 * qc + 128],
                                               sb["c_identf"][:, :]))
            W("vector", "t")
            for qc in range(4):
                V(lambda e, qc=qc: e.max(out=sb["mx8"][:, 8 * qc:8 * qc + 8], in_=pb[6][0:128, 128 * qc:128 * qc + 128]))
                V(lambda e, qc=qc: e.tensor_scalar(out=sb["sm"][:, 0:1], in0=sb["mx8"][:, 8 * qc:8 * qc + 1],
                                                   scalar1=0.25, scalar2=-1000.0, op0=ALU.mult, op1=ALU.max))
                V(lambda e: e.tensor_scalar_mul(sb["sm"][:, 1:2], sb["sm"][:, 0:1], -1.0))
                W("scalar", "v")
                W("scalar", "t", cnt["t"])
                Sc(lambda e, qc=qc: e.activation(out=sb["tq"][:, 0:128], in_=pb[6][0:128, 128 * qc:128 * qc + 128],
                                                 func=AF.Exp, bias=sb["sm"][:, 1:2], scale=0.25))
                Sc(lambda e: e.activation(out=sb["sm"][:, 3:4], in_=sb["sm"][:, 1:2],
                                          func=AF.Exp, bias=sb["cB"][:, 1:2], scale=1.0))
                W("vector", "s")
                V(lambda e: e.reduce_sum(out=sb["sm"][:, 2:3], in_=sb["tq"][:, 0:128], axis=mybir.AxisListType.X))
                V(lambda e: e.tensor_tensor(out=sb["sm"][:, 2:3], in0=sb["sm"][:, 2:3],
                                            in1=sb["sm"][:, 3:4], op=ALU.add))
                W("scalar", "v")
                Sc(lambda e: e.activation(out=sb["sm"][:, 4:5], in_=sb["sm"][:, 2:3],
                                          func=AF.Ln, bias=0.0, scale=1.0))
                W("vector", "s")
                V(lambda e: e.tensor_tensor(out=sb["sm"][:, 5:6], in0=sb["sm"][:, 0:1],
                                            in1=sb["sm"][:, 4:5], op=ALU.add))
                V(lambda e: e.tensor_scalar(out=sb["sm"][:, 5:6], in0=sb["sm"][:, 5:6],
                                            scalar1=LN10, scalar2=4.0, op0=ALU.add, op1=ALU.mult))
                V(lambda e, qc=qc: e.tensor_scalar(out=sb["tq"][:, 0:128], in0=pb[6][0:128, 128 * qc:128 * qc + 128],
                                                   scalar1=sb["mx8"][:, 8 * qc + 3:8 * qc + 4], scalar2=None, op0=ALU.is_ge))
                V(lambda e, qc=qc: e.tensor_scalar(out=sb["tq"][:, 128:256], in0=pb[6][0:128, 128 * qc:128 * qc + 128],
                                                   scalar1=sb["sm"][:, 5:6], scalar2=None, op0=ALU.is_gt))
                V(lambda e, qc=qc: e.tensor_tensor(out=sb["sel01"][:, 128 * qc:128 * qc + 128],
                                                   in0=sb["tq"][:, 0:128], in1=sb["tq"][:, 128:256], op=ALU.mult))
            W("tensor", "v")
            for qc in range(4):
                T(lambda e, qc=qc: e.transpose(pbf[6][0:128, 128 * qc:128 * qc + 128],
                                               sb["sel01"][:, 128 * qc:128 * qc + 128], sb["c_ident"][:, :]))
            W("vector", "t")
            V(lambda e: e.tensor_scalar(out=sb["selneg"][:, :], in0=pbf[6][0:128, 0:512],
                                        scalar1=1e30, scalar2=-1e30, op0=ALU.mult, op1=ALU.add))

            for g in range(4):
                r0 = 16 * kvh + 3 * g
                W("sync", "t", cnt["t"])
                DS(lambda e, kvh=kvh, g=g, half=half: e.dma_start(
                    out=sb["rqg"][:, :], in_=dr["qph"][:, 4 * kvh + g, 512 * half:512 * half + 512]))
                W("tensor", "v", cnt["v"])
                W("tensor", "s", cnt["s"])
                T(lambda e, kvh=kvh, g=g: e.matmul(pb[1][0:128, :], sb["cv_sb"][:, kvh, :],
                                                   sb["pexpg"][:, g, :], start=True, stop=False))
                T(lambda e, kvh=kvh, g=g: e.matmul(pb[1][0:128, :], sb["memvb"][:, 128 * kvh:128 * kvh + 128],
                                                   sb["pexpm"][:, g, :], start=False, stop=True))
                T(lambda e, r0=r0, g=g: e.matmul(pb[4][0:r0 + 1, :], ZOC[:, 255 - r0:256],
                                                 sb["pexpg"][:, g, :], start=False, stop=False, skip_group_check=True))
                T(lambda e, r0=r0, g=g: e.matmul(pb[4][0:r0 + 1, :], ZO1C[:, 255 - r0:256],
                                                 sb["pexpm"][:, g, :], start=False, stop=False, skip_group_check=True))
                W("tensor", "dS")
                first_pv = True
                for s_ in range(2):
                    for lt in range(4 * half + 4):
                        T(lambda e, s_=s_, lt=lt, kvh=kvh: e.matmul(
                            pb[0][0:128, :], sb["rk_sb"][:, s_, kvh, 128 * lt:128 * lt + 128],
                            sb["rqg"][:, :], start=True, stop=False))
                        T(lambda e, s_=s_, lt=lt: e.matmul(
                            pb[0][0:128, :], sb["c_emask"][:, 128 * (8 * s_ + lt):128 * (8 * s_ + lt) + 128],
                            sb["selneg"][:, :], start=False, stop=True))
                        qc = lt - 4 * half
                        if 0 <= qc < 4:
                            W("vector", "t")
                            V(lambda e, s_=s_, qc=qc: e.tensor_tensor(
                                out=pb[0][0:128, 128 * qc:128 * qc + 128],
                                in0=pb[0][0:128, 128 * qc:128 * qc + 128],
                                in1=sb["c_ownm"][:, 128 * s_:128 * s_ + 128], op=ALU.add))
                            W("scalar", "v")
                        W("scalar", "t", cnt["t"])
                        Sc(lambda e: e.activation(out=sb["pex"][:, :], in_=pb[0][0:128, :],
                                                  func=AF.Exp, bias=sb["cB"][:, 0:1], scale=SCALE))
                        W("tensor", "s")
                        T(lambda e, r0=r0: e.matmul(pb[4][0:r0 + 2, :], ZOC[:, 255 - (r0 + 1):256],
                                                    sb["pex"][:, :], start=False, stop=False, skip_group_check=True))
                        T(lambda e, s_=s_, lt=lt, kvh=kvh, first_pv=first_pv: e.matmul(
                            pb[2][0:128, :], sb["v_sb"][:, s_, lt, kvh, :], sb["pex"][:, :],
                            start=first_pv, stop=False, skip_group_check=True))
                        first_pv = False
                first_s = True
                for j, (s_, rel) in enumerate(((0, -1), (0, 0), (1, -1), (1, 0))):
                    tiles = [(qc, 4 * half + qc + rel) for qc in range(4) if 0 <= 4 * half + qc + rel < 8]
                    if not tiles:
                        continue
                    for qc, ltk in tiles:
                        T(lambda e, s_=s_, ltk=ltk, kvh=kvh, qc=qc: e.matmul(
                            pb[0][0:128, 128 * qc:128 * qc + 128],
                            sb["rk_sb"][:, s_, kvh, 128 * ltk:128 * ltk + 128],
                            sb["rqg"][:, 128 * qc:128 * qc + 128], start=True, stop=True, skip_group_check=True))
                    W("vector", "t")
                    V(lambda e, j=j: e.tensor_tensor(out=pb[0][0:128, :], in0=pb[0][0:128, :],
                                                     in1=sb["c_slm"][:, 512 * j:512 * j + 512], op=ALU.add))
                    W("scalar", "v")
                    W("scalar", "t", cnt["t"])
                    Sc(lambda e: e.activation(out=sb["pex"][:, :], in_=pb[0][0:128, :],
                                              func=AF.Exp, bias=sb["cB"][:, 0:1], scale=SCALE))
                    W("tensor", "s")
                    T(lambda e, r0=r0: e.matmul(pb[4][0:r0 + 3, :], ZOC[:, 255 - (r0 + 2):256],
                                                sb["pex"][:, :], start=False, stop=False, skip_group_check=True))
                    for qc, ltk in tiles:
                        T(lambda e, s_=s_, ltk=ltk, kvh=kvh, qc=qc, first_s=first_s: e.matmul(
                            pb[3][0:128, 128 * qc:128 * qc + 128], sb["v_sb"][:, s_, ltk, kvh, :],
                            sb["pex"][:, 128 * qc:128 * qc + 128],
                            start=first_s, stop=False, skip_group_check=True))
                    first_s = False
                W("vector", "t")
                V(lambda e: e.tensor_copy(out=sb["zs"][:, :], in_=pb[4][0:128, :]))
                W("tensor", "v")
                T(lambda e, r0=r0: e.matmul(pb[5][0:3, :], sb["c_identf"][:, r0:r0 + 3],
                                            sb["zs"][:, :], start=True, stop=True))
                T(lambda e, r0=r0, half=half: e.matmul(pb[6][0:3, :], sb["c_ident"][:, r0:r0 + 3],
                                                       sb["g_all"][:, 512 * half:512 * half + 512],
                                                       start=True, stop=True))
                W("vector", "t")
                V(lambda e: e.tensor_scalar(out=sb["scv"][0:3, :], in0=pb[5][0:3, :],
                                            scalar1=1e-30, scalar2=None, op0=ALU.add))
                V(lambda e: e.reciprocal(out=sb["scv"][0:3, :], in_=sb["scv"][0:3, :]))
                V(lambda e: e.tensor_tensor(out=sb["scv"][0:3, :], in0=sb["scv"][0:3, :],
                                            in1=pb[6][0:3, :], op=ALU.mult))
                V(lambda e: e.tensor_copy(out=sb["sclk"][0:3, :], in_=sb["scv"][0:3, :]))
                W("tensor", "v")
                for br in range(3):
                    T(lambda e, br=br: e.matmul(pb[5][0:128, :],
                                                sb["c_b3v"][0:3, 128 * br:128 * br + 128],
                                                sb["sclk"][0:3, :], start=True, stop=True, skip_group_check=True))
                    W("vector", "t")
                    bsrc = [pb[1], pb[2], pb[3]][br]
                    V(lambda e: e.tensor_copy(out=sb["m_sb"][:, :], in_=pb[5][0:128, :]))
                    if br == 0:
                        V(lambda e, bsrc=bsrc: e.tensor_tensor(out=sb["g1"], in0=bsrc[0:128, :],
                                                               in1=sb["m_sb"][:, :], op=ALU.mult))
                    elif br == 1:
                        V(lambda e, bsrc=bsrc: e.tensor_tensor(out=sb["g2"], in0=bsrc[0:128, :],
                                                               in1=sb["m_sb"][:, :], op=ALU.mult))
                        V(lambda e: e.tensor_tensor(out=sb["g1"], in0=sb["g1"],
                                                    in1=sb["g2"], op=ALU.add))
                    else:
                        V(lambda e, bsrc=bsrc: e.tensor_tensor(out=sb["g2"], in0=bsrc[0:128, :],
                                                               in1=sb["m_sb"][:, :], op=ALU.mult))
                        W("vector", "dA", cnt["dA"])
                        V(lambda e: e.tensor_tensor(out=sb["ao"][:, :], in0=sb["g1"],
                                                    in1=sb["g2"], op=ALU.add))
                    W("tensor", "v")
                W("scalar", "v")
                DA(lambda e, kvh=kvh, g=g, half=half: e.dma_start(
                    out=dr["ath"][:, 4 * kvh + g, 512 * half:512 * half + 512], in_=sb["ao"][:, :]))

    if KTRUNC < 8:
        return finish(nc, ops, cnt, W, DS, sb, out_hbm, block_emit)
    def wo_evac(oc, th, bank):
        W("scalar", "v", cnt["v"])
        DA(lambda e, oc=oc, th=th: e.dma_start(out=sb["xsq"][:, :],
                                               in_=dr["x1h"][:, oc, 512 * th:512 * th + 512]))
        W("vector", "t")
        W("vector", "dA")
        V(lambda e, bank=bank: e.tensor_copy(out=sb["t32"][:, 0, :], in_=bank[0:128, :]))
        V(lambda e: e.tensor_tensor(out=sb["vtmp"][:, :], in0=sb["t32"][:, 0, :],
                                    in1=sb["xsq"][:, :], op=ALU.add))
        W("scalar", "v")
        DA(lambda e, oc=oc, th=th: e.dma_start(out=dr["x2h"][:, oc, 512 * th:512 * th + 512],
                                               in_=sb["vtmp"][:, :]))

    stream_mm_dram("wo", 8, 32, "ath", wo_evac)

    if KTRUNC < 9:
        return finish(nc, ops, cnt, W, DS, sb, out_hbm, block_emit)
    layernorm("x2h", "lnh")
    stream_mm_dram("ffn", 8, 32, "lnh", ffn_evac("x2h", None, final=True))

    return finish(nc, ops, cnt, W, DS, sb, out_hbm, block_emit, full=True)


def block_emit(nc, ops):
    with nc.Block() as block:
        semh = {}
        for s_ in ("t", "v", "s", "dS", "dG", "dA", "cc"):
            semh[s_] = nc.alloc_semaphore("sem_" + s_)

        def emit(engine):
            def body(eng):
                for item in ops[engine]:
                    if item[0] == "w":
                        eng.wait_ge(semh[item[1]], item[2])
                    else:
                        _, fn, inc, by = item
                        ins = fn(eng)
                        if inc is not None and ins is not None:
                            ins.then_inc(semh[inc], by)
            return body

        block.tensor(emit("tensor"))
        block.vector(emit("vector"))
        block.scalar(emit("scalar"))
        block.gpsimd(emit("gpsimd"))
        block.sync(emit("sync"))
    return nc


def finish(nc, ops, cnt, W, DS, sb, out_hbm, emitter, full=False):
    if not full:
        for oc in range(32):
            DS(lambda e, oc=oc: e.dma_start(out=out_hbm[:, oc, :],
                                            in_=sb["zs"][:, :].to_broadcast([128, 1024]) if False else sb["AR1"][:, 0:1024].bitcast(F32)[:, 0:512].to_broadcast([128, 1024]) if False else sb["mubc"][:, :].to_broadcast([128, 1024]) if False else sb["AR1"].bitcast(F32)[:, 0:1024]))
    W("sync", "dA")
    W("sync", "dS")
    W("sync", "dG")
    W("sync", "cc")
    return emitter(nc, ops)


def _consts(par):
    c = {}
    c["ident"] = np.eye(128, dtype=np.float32).astype(BF)
    c["identf"] = np.eye(128, dtype=np.float32)
    c["ones1f"] = np.ones((1, 128), np.float32)
    swp = np.zeros((128, 128), np.float32)
    for f in range(64):
        swp[2 * f + 1, 2 * f] = 1.0
        swp[2 * f, 2 * f + 1] = 1.0
    c["swp"] = swp.astype(BF)
    st = np.zeros((128, 4), np.float32)
    st[:, 0] = 1.0
    st[:, 3] = 1.0
    c["stat2"] = st.astype(BF)
    c["ones1"] = np.ones((1, 128), np.float32).astype(BF)
    zo = np.zeros((128, 256), np.float32)
    zo[:, 255] = 0.0
    zo[:, 255] = 1.0
    c["zo"] = zo.astype(BF)
    zo1 = np.zeros((1, 256), np.float32)
    zo1[:, 255] = 1.0
    c["zo1"] = zo1.astype(BF)
    b3 = np.zeros((3, 384), np.float32)
    for br in range(3):
        b3[br, 128 * br:128 * br + 128] = 1.0
    c["b3v"] = b3.astype(BF)
    em = np.zeros((128, 2048), np.float32)
    for v in range(16):
        for k in range(128):
            em[v * 8 + k // 16, v * 128 + k] = 1.0
    c["emask"] = em.astype(BF)
    cm = np.zeros((128, TOK), np.float32)
    qpos = np.zeros(TOK, np.int64)
    for lc in range(8):
        qpos[128 * lc:128 * lc + 128] = (2 * lc + par) * 128 + np.arange(128)
    for s in range(2):
        for lb in range(64):
            sqb = (2 * (lb // 8) + s) * 8 + lb % 8
            cm[s * 64 + lb, :] = np.where(sqb * 16 + 15 < qpos, 0.0, -1e30)
    c["cmask"] = cm.astype(BF)
    own = np.zeros((128, 256), np.float32)
    pat = np.zeros((128, 128), np.float32)
    for k in range(128):
        for q in range(128):
            if k // 16 == q // 16 and k % 16 <= q % 16:
                pat[k, q] = 1e30
    own[:, 128 * par:128 * par + 128] = pat
    c["ownm"] = own.astype(BF)
    bandB = np.zeros((128, 128), np.float32)
    bandA = np.zeros((128, 128), np.float32)
    for k in range(128):
        for q in range(128):
            bandB[k, q] = 0.0 if (q - 64 <= k <= q) else -1e30
            bandA[k, q] = 0.0 if (k >= q + 64) else -1e30
    negm = np.full((128, 128), -1e30, np.float32)
    if par == 0:
        mdict = {(0, -1): negm, (0, 0): bandB, (1, -1): bandA, (1, 0): negm}
    else:
        mdict = {(0, -1): negm, (0, 0): bandA, (1, -1): negm, (1, 0): bandB}
    slm = np.zeros((128, 2048), np.float32)
    for j, key in enumerate(((0, -1), (0, 0), (1, -1), (1, 0))):
        slm[:, 512 * j:512 * j + 512] = np.tile(mdict[key], (1, 4))
    c["slm"] = slm.astype(BF)
    inv = 1.0 / 10000.0 ** (np.arange(0, DH, 2, dtype=np.float64) / DH)
    ang = qpos[None, :].astype(np.float64) * inv[:, None]
    cosF = np.zeros((128, TOK), np.float64)
    sinF = np.zeros((128, TOK), np.float64)
    cosF[0::2, :] = np.cos(ang)
    cosF[1::2, :] = np.cos(ang)
    sinF[0::2, :] = -np.sin(ang)
    sinF[1::2, :] = np.sin(ang)
    c["cosF"] = cosF.astype(np.float32)
    c["sinF"] = sinF.astype(np.float32)
    return c


def kernel(L0_attn_hidden_states, L0_attn_w_qkv, L0_attn_w_o, L0_attn_k_pos,
           L0_attn_v_pos, L0_attn_k_w1, L0_attn_k_b1, L0_attn_k_w2, L0_attn_k_b2,
           L0_attn_v_w1, L0_attn_v_b1, L0_attn_v_w2, L0_attn_v_b2, L0_attn_mem_kv,
           L0_attn_w_gate, L0_attn_b_gate, L0_ffn_q_scales, L0_ffn_q_vals, L0_ffn_bias):
    hs = np.ascontiguousarray(L0_attn_hidden_states)
    nc = build_nc()
    in_maps = []
    for core in range(8):
        b, par = core // 2, core % 2
        idx = np.concatenate([np.arange(128) + (2 * lc + par) * 128 for lc in range(8)])
        m = {}
        xf = hs[b][idx].T
        m["x_in"] = np.ascontiguousarray(xf.reshape(32, 128, TOK).transpose(1, 0, 2))
        qvs = np.ascontiguousarray(L0_ffn_q_vals[512 * core:512 * core + 512])
        m["qv"] = np.ascontiguousarray(qvs.reshape(4, 128, 4096).transpose(1, 0, 2))
        m["qs"] = np.ascontiguousarray(L0_ffn_q_scales[512 * core:512 * core + 512].reshape(4, 128, 128).transpose(1, 0, 2))
        m["wqkv_s"] = np.ascontiguousarray(L0_attn_w_qkv[768 * core:768 * core + 768])
        m["wo_s"] = np.ascontiguousarray(L0_attn_w_o[512 * core:512 * core + 512])
        m["kw1_s"] = np.ascontiguousarray(L0_attn_k_w1[256 * core:256 * core + 256])
        m["vw1_s"] = np.ascontiguousarray(L0_attn_v_w1[256 * core:256 * core + 256])
        wg = np.zeros((128, D), np.float32)
        bg = np.zeros((128, 1), np.float32)
        for kvh in range(KVH):
            for g in range(G):
                for j in range(3):
                    wg[16 * kvh + 3 * g + j] = L0_attn_w_gate[3 * (4 * kvh + g) + j]
                    bg[16 * kvh + 3 * g + j, 0] = L0_attn_b_gate[3 * (4 * kvh + g) + j]
        m["wg"], m["bgate"] = wg, bg
        m["kw2"] = np.ascontiguousarray(L0_attn_k_w2)
        m["vw2"] = np.ascontiguousarray(L0_attn_v_w2)
        m["kb1"] = np.ascontiguousarray(L0_attn_k_b1.reshape(16, 128).T)
        m["vb1"] = np.ascontiguousarray(L0_attn_v_b1.reshape(16, 128).T)
        m["kb2"] = np.ascontiguousarray(L0_attn_k_b2.reshape(128, 1))
        m["vb2"] = np.ascontiguousarray(L0_attn_v_b2.reshape(128, 1))
        m["ffnb"] = np.ascontiguousarray(L0_ffn_bias.reshape(32, 128).T)
        m["kpos"] = np.ascontiguousarray(L0_attn_k_pos.transpose(2, 0, 1).reshape(128, 128))
        m["vpos"] = np.ascontiguousarray(L0_attn_v_pos.transpose(2, 0, 1).reshape(128, 128))
        m["memk"] = np.ascontiguousarray(L0_attn_mem_kv[0, :, 0, :].T)
        m["memv"] = np.ascontiguousarray(L0_attn_mem_kv[1, :, 0, :].reshape(1, 1024))
        cc = _consts(par)
        for k_, v_ in cc.items():
            m[k_] = v_
        in_maps.append(m)
    res = run_bass_kernel_spmd(nc, in_maps, core_ids=list(range(8)))
    out = np.zeros((4, 2048, 4096), np.float32)
    for core in range(8):
        b, par = core // 2, core % 2
        idx = np.concatenate([np.arange(128) + (2 * lc + par) * 128 for lc in range(8)])
        o = res.results[core]["out"]
        of = o.transpose(1, 0, 2).reshape(4096, TOK)
        out[b][idx] = of.T
    return out


# version 07 (final)
# speedup vs baseline: 1.5275x; 1.5275x over previous
import os
import numpy as np
import ml_dtypes
import concourse.bass as bass
import concourse.mybir as mybir
from concourse.bass_utils import run_bass_kernel_spmd

F32, BF16, I32 = mybir.dt.float32, mybir.dt.bfloat16, mybir.dt.int32
AF, ALU = mybir.ActivationFunctionType, mybir.AluOpType
BF = ml_dtypes.bfloat16
TOK, D, KVH, G, DH = 1024, 4096, 8, 4, 128
SCALE = DH ** -0.5
LN10 = float(np.log(1e-10))


KTRUNC = int(os.environ.get("KTRUNC", "99"))


def build_nc():
    nc = bass.Bass("TRN2", target_bir_lowering=False)
    dr, sb = {}, {}

    def din(n, shp, dt):
        dr[n] = nc.dram_tensor(n, list(shp), dt, kind="ExternalInput").ap()

    def dint(n, shp, dt):
        dr[n] = nc.dram_tensor(n, list(shp), dt, kind="Internal").ap()

    def asb(n, shp, dt):
        sb[n] = nc.alloc_sbuf_tensor(n, list(shp), dt).ap()

    din("x_in", (128, 32, TOK), F32)
    din("qv", (128, 4, 4096), I32)
    din("qs", (128, 4, 128), F32)
    din("wqkv_s", (768, D), F32)
    din("wo_s", (512, D), F32)
    din("kw1_s", (256, 2048), F32)
    din("vw1_s", (256, 2048), F32)
    din("wg", (128, D), F32)
    din("kw2", (128, 2048), F32)
    din("vw2", (128, 2048), F32)
    din("ones1f", (1, 128), F32)
    din("identf", (128, 128), F32)
    for n, shp in (("kb1", (128, 16)), ("vb1", (128, 16)), ("kb2", (128, 1)),
                   ("vb2", (128, 1)), ("bgate", (128, 1)), ("ffnb", (128, 32)),
                   ("kpos", (128, 128)), ("vpos", (128, 128)), ("memk", (128, 8)),
                   ("memv", (1, 1024)), ("cosF", (128, TOK)), ("sinF", (128, TOK))):
        din(n, shp, F32)
    for n, shp in (("ident", (128, 128)), ("swp", (128, 128)), ("stat2", (128, 4)),
                   ("ones1", (1, 128)), ("zo", (128, 256)), ("zo1", (1, 256)),
                   ("b3v", (3, 384)), ("emask", (128, 2048)), ("cmask", (128, TOK)),
                   ("ownm", (128, 256)), ("slm", (128, 2048))):
        din(n, shp, BF16)
    out_hbm = nc.dram_tensor("out", [128, 32, TOK], F32, kind="ExternalOutput").ap()

    dint("wc_f", (4, 128, D), BF16)
    dint("wc_r", (12, 128, D), BF16)
    dint("wg_f", (8, 512, D), BF16)
    dint("wg_r", (8, 1536, D), BF16)
    KVE = 128 * 8192 + 128 * 8192 + 128 * 512 + 64 * 1024
    dint("kc_in", (KVE,), BF16)
    dint("kc_out", (2, KVE), BF16)
    for n in ("x0h", "x1h", "x2h", "lnh", "qrh", "qph", "ath"):
        dint(n, (128, 32, TOK), BF16)
    dint("wgl", (128, 4096), BF16)
    dint("kw2l", (128, 2048), BF16)
    dint("vw2l", (128, 2048), BF16)

    asb("mv", (128, 4, 512), BF16)
    asb("wst", (128, 2, 512), BF16)
    asb("AR1", (128, 32768), BF16)
    asb("xsq", (128, 512), BF16)
    asb("mu", (1, 512), F32)
    asb("rr", (1, 512), F32)
    asb("mubc", (128, 512), F32)
    asb("rbc", (128, 512), F32)
    asb("t32", (128, 2, 512), F32)
    asb("X1", (128, 8192), BF16)
    asb("X2", (128, 8192), BF16)
    asb("X3", (128, 8192), BF16)
    asb("X4", (128, 8192), BF16)
    asb("ckf", (128, 8, 64), BF16)
    asb("cvf", (128, 8, 128), BF16)
    asb("vtmp", (128, 512), BF16)
    asb("g_all", (128, TOK), BF16)
    asb("ck_sb", (128, 8, 2, 64), BF16)
    asb("cv_sb", (128, 8, 128), BF16)
    asb("qg", (128, 512), BF16)
    asb("rqg", (128, 512), BF16)
    asb("pexpg", (128, 4, 512), BF16)
    asb("pexpm", (1, 4, 512), BF16)
    asb("m_sb", (128, 512), F32)
    asb("sel01", (128, 512), BF16)
    asb("selneg", (128, 512), BF16)
    asb("mx8", (128, 32), F32)
    asb("sm", (128, 8), F32)
    asb("cB", (128, 2), F32)
    asb("tq", (128, 256), F32)
    asb("zs", (128, 512), F32)
    asb("sclk", (3, 512), BF16)
    asb("scb", (128, 512), BF16)
    asb("pex", (128, 512), BF16)
    asb("ao", (128, 512), BF16)
    asb("ost", (128, 1, 512), F32)
    asb("memkb", (128, 8), BF16)
    asb("memvb", (1, 1024), BF16)
    CON_F32 = ("kb1", "vb1", "kb2", "vb2", "bgate", "ffnb", "kpos", "vpos",
               "ones1f", "identf")
    CON_BF = ("ident", "swp", "stat2", "ones1", "zo", "zo1", "b3v", "emask",
              "cmask", "ownm", "slm")
    for n in CON_F32:
        asb("c_" + n, dr[n].shape, F32)
    for n in CON_BF:
        asb("c_" + n, dr[n].shape, BF16)
    asb("c_cosF", (128, TOK), BF16)
    asb("c_sinF", (128, TOK), BF16)
    asb("ffnbh", (128, 32), F32)
    pb = [nc.alloc_psum_tensor(f"pb{i}", [128, 512], F32).ap() for i in range(8)]
    sb["st32"] = sb["AR1"][:, 0:16384].bitcast(F32).rearrange("p (a b) -> p a b", a=2)
    sb["stbf"] = sb["AR1"][:, 16384:24576].rearrange("p (a b) -> p a b", a=2)
    sb["rk_sb"] = sb["AR1"][:, 0:16384].rearrange("p (a b c) -> p a b c", a=2, b=8)
    sb["v_sb"] = sb["AR1"][:, 16384:32768].rearrange("p (a b c d) -> p a b c d", a=2, b=8, c=8)
    sb["k_sb"] = sb["X1"].rearrange("p (a b) -> p a b", a=8)
    sb["hid"] = sb["X1"].rearrange("p (a b) -> p a b", a=16)
    sb["rk_own"] = sb["X2"].rearrange("p (a b) -> p a b", a=8)
    sb["vpv"] = sb["X2"].rearrange("p (a b c) -> p a b c", a=8, b=64)
    sb["vf"] = sb["X3"].rearrange("p (a b) -> p a b", a=8)
    sb["kpk"] = sb["X4"].rearrange("p (a b c) -> p a b c", a=8, b=64)
    sb["qs_sb"] = sb["wst"][:, :, :].rearrange("p a b -> p (a b)").bitcast(F32)
    sb["g1"] = sb["t32"][:, 0, :]
    sb["g2"] = sb["t32"][:, 1, :]
    sb["scv"] = sb["ost"][:, 0, :]
    pbf = [p.bitcast(BF16) for p in pb]

    ops = {e: [] for e in ("tensor", "vector", "scalar", "gpsimd", "sync")}
    cnt = {k: 0 for k in ("t", "v", "s", "dS", "dT", "dG", "dA", "cc")}

    def W(eng, sem, val=None):
        v = cnt[sem] if val is None else val
        if v > 0:
            ops[eng].append(("w", sem, v))

    def mk(eng, sem, by=1):
        def add(fn):
            cnt[sem] += by
            ops[eng].append(("o", fn, sem, by))
        return add

    T = mk("tensor", "t")
    _V, _Sc = mk("vector", "v"), mk("scalar", "s")

    def V(fn):
        W("vector", "v", cnt["v"])
        _V(fn)

    def Sc(fn):
        W("scalar", "s", cnt["s"])
        _Sc(fn)
    _DS, _DG, _DA = mk("sync", "dS", 16), mk("gpsimd", "dG", 16), mk("scalar", "dA", 16)
    DT = mk("sync", "dT", 16)

    def DS(fn):
        W("sync", "dS", cnt["dS"])
        W("sync", "dT", cnt["dT"])
        _DS(fn)

    def DG(fn):
        W("gpsimd", "dG", cnt["dG"])
        _DG(fn)

    def DA(fn):
        W("scalar", "dA", cnt["dA"])
        _DA(fn)
    CC = mk("gpsimd", "cc", 1)

    for n in CON_F32 + CON_BF:
        DS(lambda e, n=n: e.dma_start(out=sb["c_" + n][tuple(slice(None) for _ in sb["c_" + n].shape)],
                                      in_=dr[n][tuple(slice(None) for _ in dr[n].shape)]))
    DG(lambda e: e.dma_start(out=sb["memvb"][:], in_=dr["memv"][:, :]))
    DG(lambda e: e.dma_start(out=sb["memkb"][:], in_=dr["memk"][:, :]))
    DG(lambda e: e.dma_start(out=sb["c_cosF"][:], in_=dr["cosF"][:, :]))
    DG(lambda e: e.dma_start(out=sb["c_sinF"][:], in_=dr["sinF"][:, :]))
    W("vector", "dS")
    W("vector", "dG")
    V(lambda e: e.tensor_scalar_mul(sb["ffnbh"][:], sb["c_ffnb"][:], 0.5))
    V(lambda e: e.memset(sb["cB"][:, 0:1], -32.0))
    V(lambda e: e.memset(sb["cB"][:, 1:2], -1000.0))

    W("gpsimd", "dG")
    for i in range(8):
        W("gpsimd", "v", cnt["v"])
        DG(lambda e, i=i: e.dma_start(out=sb["st32"][:, i % 2, :],
                                      in_=dr["x_in"][:, 4 * i:4 * i + 4, :].rearrange("p a b -> p (a b)")))
        W("vector", "dG")
        W("vector", "dA", cnt["dA"])
        V(lambda e, i=i: e.tensor_copy(
            out=sb["stbf"][:, i % 2, :],
            in_=sb["st32"][:, i % 2, :]))
        W("scalar", "v")
        DA(lambda e, i=i: e.dma_start(out=dr["x0h"][:, 4 * i:4 * i + 4, :].rearrange("p a b -> p (a b)"),
                                      in_=sb["stbf"][:, i % 2, :]))

    stbff = sb["stbf"].rearrange("p a b -> p (a b)")
    st32v = sb["st32"]
    for t in range(4):
        W("sync", "s", cnt["s"])
        W("sync", "v", cnt["v"])
        DS(lambda e, t=t: e.dma_start(out=st32v[:, t % 2, :].bitcast(I32), in_=dr["qv"][:, t, :]))
        DS(lambda e, t=t: e.dma_start(out=sb["qs_sb"][:, 128 * (t % 2):128 * (t % 2) + 128], in_=dr["qs"][:, t, :]))
        W("vector", "dS")
        W("vector", "dA", cnt["dA"])
        V(lambda e, t=t: e.scalar_tensor_tensor(
            out=stbff[:, 4096 * (t % 2):4096 * (t % 2) + 4096].rearrange("p (b j) -> p b j", j=32),
            in0=st32v[:, t % 2, :].bitcast(I32).rearrange("p (b j) -> p b j", j=32),
            scalar=128.0,
            in1=sb["qs_sb"][:, 128 * (t % 2):128 * (t % 2) + 128].unsqueeze(2).to_broadcast([128, 128, 32]),
            op0=ALU.subtract, op1=ALU.mult))
        W("scalar", "v")
        DA(lambda e, t=t: e.dma_start(out=dr["wc_f"][t, :, :],
                                      in_=stbff[:, 4096 * (t % 2):4096 * (t % 2) + 4096]))
    RS = [("wqkv_s", 0, 6, 4096), ("wo_s", 6, 4, 4096), ("kw1_s", 10, 1, 2048), ("vw1_s", 11, 1, 2048)]
    for src, base, nt, cols in RS:
        for t in range(nt):
            W("sync", "s", cnt["s"])
            W("sync", "v", cnt["v"])
            if cols == 4096:
                DS(lambda e, src=src, t=t: e.dma_start(out=st32v[:, t % 2, :],
                                                       in_=dr[src][128 * t:128 * t + 128, :]))
            else:
                DS(lambda e, src=src, t=t: e.dma_start(out=st32v[:, t % 2, 0:2048],
                                                       in_=dr[src][128 * (2 * t):128 * (2 * t) + 128, :]))
                DS(lambda e, src=src, t=t: e.dma_start(out=st32v[:, t % 2, 2048:4096],
                                                       in_=dr[src][128 * (2 * t + 1):128 * (2 * t + 1) + 128, :]))
            W("vector", "dS")
            W("vector", "dA", cnt["dA"])
            V(lambda e, t=t: e.tensor_copy(out=stbff[:, 4096 * (t % 2):4096 * (t % 2) + 4096],
                                           in_=st32v[:, t % 2, :]))
            W("scalar", "v")
            DA(lambda e, base=base, t=t: e.dma_start(out=dr["wc_r"][base + t, :, :],
                                                     in_=stbff[:, 4096 * (t % 2):4096 * (t % 2) + 4096]))

    for srcd, dstd in (("kw2", "kw2l"), ("vw2", "vw2l"), ("wg", "wgl")):
        ncols = dr[srcd].shape[1]
        W("sync", "dA")
        W("sync", "v", cnt["v"])
        DS(lambda e, srcd=srcd, ncols=ncols: e.dma_start(
            out=sb["st32"][:, 0, 0:ncols], in_=dr[srcd][:, 0:ncols]))
        W("vector", "dS")
        W("vector", "dA", cnt["dA"])
        V(lambda e, ncols=ncols: e.tensor_copy(out=sb["stbf"][:, 0, 0:ncols],
                                               in_=sb["st32"][:, 0, 0:ncols]))
        W("scalar", "v")
        DA(lambda e, dstd=dstd, ncols=ncols: e.dma_start(out=dr[dstd][:, 0:ncols],
                                                         in_=sb["stbf"][:, 0, 0:ncols]))
    W("gpsimd", "dA")
    W("gpsimd", "dS")
    CC(lambda e: e.collective_compute("AllGather", ALU.bypass,
                                      replica_groups=[[0, 1, 2, 3, 4, 5, 6, 7]],
                                      ins=[dr["wc_f"][:, :, :].rearrange("a p f -> (a p f)")],
                                      outs=[dr["wg_f"][:, :, :].rearrange("a p f -> (a p f)")]))
    CC(lambda e: e.collective_compute("AllGather", ALU.bypass,
                                      replica_groups=[[0, 1, 2, 3, 4, 5, 6, 7]],
                                      ins=[dr["wc_r"][:, :, :].rearrange("a p f -> (a p f)")],
                                      outs=[dr["wg_r"][:, :, :].rearrange("a p f -> (a p f)")]))

    def layernorm(src, dst):
        W("sync", "dA")
        for th in range(2):
            for ic in range(32):
                buf = ic % 2
                W("sync", "t", cnt["t"])
                W("sync", "s", cnt["s"])
                W("sync", "dA", cnt["dA"])
                DS(lambda e, ic=ic, th=th, buf=buf: e.dma_start(
                    out=sb["mv"][:, buf, :], in_=dr[src][:, ic, 512 * th:512 * th + 512]))
                W("scalar", "dS")
                W("scalar", "v", cnt["v"])
                Sc(lambda e, buf=buf: e.activation(out=sb["xsq"][:, :], in_=sb["mv"][:, buf, :],
                                                   func=AF.Square, bias=0.0, scale=1.0))
                W("tensor", "s")
                W("tensor", "dS")
                T(lambda e, ic=ic, th=th, buf=buf: e.matmul(pb[6][0:1, :], sb["c_stat2"][:, 0:1],
                                                            sb["mv"][:, buf, :], start=(ic == 0), stop=(ic == 31)))
                T(lambda e, ic=ic: e.matmul(pb[5][0:1, :], sb["c_stat2"][:, 0:1], sb["xsq"][:, :],
                                            start=(ic == 0), stop=(ic == 31)))
            W("vector", "t")
            V(lambda e: e.tensor_scalar_mul(sb["mu"][:, :], pb[6][0:1, :], 1.0 / 4096))
            V(lambda e: e.tensor_scalar_mul(sb["zs"][0:1, 0:512], pb[5][0:1, :], 1.0 / 4096))
            V(lambda e: e.tensor_tensor(out=sb["g1"][0:1, :], in0=sb["mu"][:, :],
                                        in1=sb["mu"][:, :], op=ALU.mult))
            V(lambda e: e.tensor_tensor(out=sb["zs"][0:1, 0:512], in0=sb["zs"][0:1, 0:512],
                                        in1=sb["g1"][0:1, :], op=ALU.subtract))
            V(lambda e: e.tensor_scalar_add(sb["zs"][0:1, 0:512], sb["zs"][0:1, 0:512], 1e-5))
            W("scalar", "v")
            Sc(lambda e: e.activation(out=sb["zs"][0:1, 0:512], in_=sb["zs"][0:1, 0:512],
                                      func=AF.Sqrt, bias=0.0, scale=1.0))
            W("vector", "s")
            V(lambda e: e.reciprocal(out=sb["zs"][0:1, 0:512], in_=sb["zs"][0:1, 0:512]))
            V(lambda e: e.tensor_copy(out=sb["rr"][:, :], in_=sb["zs"][0:1, 0:512]))
            W("tensor", "v")
            T(lambda e: e.matmul(pb[6][0:128, :], sb["c_ones1f"][:, :],
                                 sb["mu"][0:1, :], start=True, stop=True))
            T(lambda e: e.matmul(pb[7][0:128, :], sb["c_ones1f"][:, :],
                                 sb["rr"][0:1, :], start=True, stop=True))
            W("vector", "t")
            V(lambda e: e.tensor_copy(out=sb["mubc"][:, :], in_=pb[6][0:128, :]))
            V(lambda e: e.tensor_copy(out=sb["rbc"][:, :], in_=pb[7][0:128, :]))
            for ic in range(32):
                buf = ic % 2
                W("sync", "v", cnt["v"])
                W("sync", "dA", cnt["dA"])
                DS(lambda e, ic=ic, th=th, buf=buf: e.dma_start(
                    out=sb["mv"][:, buf, :], in_=dr[src][:, ic, 512 * th:512 * th + 512]))
                W("vector", "dS")
                V(lambda e, buf=buf: e.tensor_tensor(out=sb["t32"][:, 0, :], in0=sb["mv"][:, buf, :],
                                                     in1=sb["mubc"][:, :], op=ALU.subtract))
                V(lambda e, buf=buf: e.tensor_tensor(out=sb["mv"][:, buf, :], in0=sb["t32"][:, 0, :],
                                                     in1=sb["rbc"][:, :], op=ALU.mult))
                W("scalar", "v")
                DA(lambda e, ic=ic, th=th, buf=buf: e.dma_start(
                    out=dr[dst][:, ic, 512 * th:512 * th + 512], in_=sb["mv"][:, buf, :]))

    def wsrc_pieces(kind, og, ic):
        if kind == "ffn":
            return [(dr["wg_f"][og, :, 128 * ic:128 * ic + 128], 0, 512)]
        if kind == "qkv":
            r0, out = 512 * og, []
            col = 0
            while col < 512:
                r = r0 + col
                c, lr = divmod(r, 768)
                n = min(512 - col, 768 - lr)
                out.append((dr["wg_r"][c, lr:lr + n, 128 * ic:128 * ic + 128], col, n))
                col += n
            return out
        if kind == "wo":
            return [(dr["wg_r"][og, 768:1280, 128 * ic:128 * ic + 128], 0, 512)]
        if kind in ("kw1", "vw1"):
            r0 = 1280 if kind == "kw1" else 1408
            out = []
            for j, c2 in enumerate((2 * og, 2 * og + 1)):
                for h_ in range(2):
                    out.append((dr["wg_r"][c2, r0:r0 + 128,
                                           2048 * h_ + 128 * ic:2048 * h_ + 128 * ic + 128],
                                256 * j + 128 * h_, 128))
            return out

    def stream_mm(kind, nog, nic, moving, evac, nth=2):
        for og in range(nog):
            for ic in range(nic):
                buf = ic % 2
                W("sync", "t", cnt["t"])
                for ap_, col0, ncols in wsrc_pieces(kind, og, ic):
                    DS(lambda e, ap_=ap_, col0=col0, buf=buf: e.dma_start(
                        out=sb["wst"][:, buf, col0:col0 + ap_.shape[0]], in_=ap_, transpose=True))
                W("tensor", "dS")
                for ol in range(4):
                    for th in range(nth):
                        bank = pb[2 * ol + th]
                        T(lambda e, ic=ic, ol=ol, th=th, buf=buf, bank=bank, nic=nic: e.matmul(
                            bank[0:128, :], sb["wst"][:, buf, 128 * ol:128 * ol + 128],
                            moving(ic, th), start=(ic == 0), stop=(ic == nic - 1)))
            for ol in range(4):
                for th in range(nth):
                    evac(4 * og + ol, th, pb[2 * ol + th])

    def mload(src):
        def f(ic, th):
            return sb["mv"][:, (2 * ic + th) % 2, :]
        def prefetch(ic, th):
            buf = (2 * ic + th) % 2
            W("sync", "t", cnt["t"])
            DS(lambda e, ic=ic, th=th, buf=buf: e.dma_start(
                out=sb["mv"][:, buf, :], in_=dr[src][:, ic, 512 * th:512 * th + 512]))
            W("tensor", "dS")
        return f, prefetch

    def stream_mm_dram(kind, nog, nic, src, evac, nth=2):
        W("sync", "dA")
        W("sync", "cc")
        for og in range(nog):
            W("tensor", "v", cnt["v"])
            W("tensor", "s", cnt["s"])
            thist = []
            for ic in range(nic):
                thist.append(cnt["t"])
                if ic >= 2:
                    W("sync", "t", thist[ic - 1])
                else:
                    W("sync", "t", thist[0])
                EM = _DS if ic % 2 == 0 else DT
                psem = "dS" if ic % 2 == 0 else "dT"
                for th in range(nth):
                    buf = 2 * (ic % 2) + th
                    EM(lambda e, ic=ic, th=th, buf=buf, src=src: e.dma_start(
                        out=sb["mv"][:, buf, :], in_=dr[src][:, ic, 512 * th:512 * th + 512]))
                bufw = ic % 2
                for ap_, col0, ncols in wsrc_pieces(kind, og, ic):
                    EM(lambda e, ap_=ap_, col0=col0, bufw=bufw: e.dma_start(
                        out=sb["wst"][:, bufw, col0:col0 + ap_.shape[0]], in_=ap_, transpose=True))
                W("tensor", psem)
                for ol in range(4):
                    for th in range(nth):
                        T(lambda e, ic=ic, ol=ol, th=th, bufw=bufw, nic=nic: e.matmul(
                            pb[2 * ol + th][0:128, :], sb["wst"][:, bufw, 128 * ol:128 * ol + 128],
                            sb["mv"][:, 2 * (ic % 2) + th, :], start=(ic == 0), stop=(ic == nic - 1)))
            for ol in (3, 0, 1, 2):
                for th in range(nth):
                    evac(4 * og + ol, th, pb[2 * ol + th])

    if KTRUNC < 1:
        return finish(nc, ops, cnt, W, DS, sb, out_hbm, block_emit)
    W("sync", "cc", 1)
    W("sync", "dA")
    layernorm("x0h", "lnh")

    def ffn_evac(xsrc, xdst, final=False):
        def evac(oc, th, bank):
            W("scalar", "v", cnt["v"])
            DA(lambda e, oc=oc, th=th: e.dma_start(out=sb["xsq"][:, :],
                                                   in_=dr[xsrc][:, oc, 512 * th:512 * th + 512]))
            W("vector", "t")
            W("vector", "dA")
            V(lambda e, oc=oc, bank=bank: e.tensor_scalar(
                out=sb["t32"][:, 0, :], in0=bank[0:128, :], scalar1=0.5,
                scalar2=sb["ffnbh"][:, oc:oc + 1], op0=ALU.mult, op1=ALU.add))
            if final:
                V(lambda e, oc=oc, th=th: e.tensor_tensor(
                    out=sb["ost"][:, 0, :], in0=sb["t32"][:, 0, :],
                    in1=sb["xsq"][:, :], op=ALU.add))
                W("scalar", "v")
                DA(lambda e, oc=oc, th=th: e.dma_start(
                    out=out_hbm[:, oc, 512 * th:512 * th + 512], in_=sb["ost"][:, 0, :]))
            else:
                V(lambda e, oc=oc, th=th: e.tensor_tensor(
                    out=sb["xsq"][:, :], in0=sb["t32"][:, 0, :],
                    in1=sb["xsq"][:, :], op=ALU.add))
                W("scalar", "v")
                DA(lambda e, oc=oc, th=th: e.dma_start(
                    out=dr[xdst][:, oc, 512 * th:512 * th + 512], in_=sb["xsq"][:, :]))
        return evac

    if KTRUNC >= 2:
        stream_mm_dram("ffn", 8, 32, "lnh", ffn_evac("x0h", "x1h"))
    if KTRUNC < 3:
        return finish(nc, ops, cnt, W, DS, sb, out_hbm, block_emit)

    layernorm("x1h", "lnh")
    if KTRUNC < 4:
        return finish(nc, ops, cnt, W, DS, sb, out_hbm, block_emit)

    def qkv_evac(oc, th, bank):
        if oc < 32:
            W("vector", "t")
            W("vector", "dA", cnt["dA"])
            V(lambda e, bank=bank: e.tensor_copy(out=sb["qg"][:, :], in_=bank[0:128, :]))
            W("scalar", "v")
            DA(lambda e, oc=oc, th=th: e.dma_start(out=dr["qrh"][:, oc, 512 * th:512 * th + 512],
                                                   in_=sb["qg"][:, :]))
            W("tensor", "v")
            T(lambda e: e.matmul(pb[6][0:128, :], sb["c_swp"][:, :], sb["qg"][:, :], start=True, stop=True))
            W("vector", "t")
            V(lambda e, th=th: e.tensor_tensor(out=sb["t32"][:, 0, :], in0=sb["qg"][:, :],
                                               in1=sb["c_cosF"][:, 512 * th:512 * th + 512], op=ALU.mult))
            V(lambda e, th=th: e.tensor_tensor(out=sb["t32"][:, 1, :], in0=pb[6][0:128, :],
                                               in1=sb["c_sinF"][:, 512 * th:512 * th + 512], op=ALU.mult))
            W("vector", "dA", cnt["dA"])
            V(lambda e: e.tensor_tensor(out=sb["qg"][:, :], in0=sb["t32"][:, 0, :],
                                        in1=sb["t32"][:, 1, :], op=ALU.add))
            W("scalar", "v")
            DA(lambda e, oc=oc, th=th: e.dma_start(out=dr["qph"][:, oc, 512 * th:512 * th + 512],
                                                   in_=sb["qg"][:, :]))
        elif oc < 40:
            kvh = oc - 32
            W("vector", "t")
            V(lambda e, kvh=kvh, th=th, bank=bank: e.tensor_copy(
                out=sb["k_sb"][:, kvh, 512 * th:512 * th + 512], in_=bank[0:128, :]))
            W("tensor", "v")
            T(lambda e, kvh=kvh, th=th: e.matmul(pb[6][0:128, :], sb["c_swp"][:, :],
                                                 sb["k_sb"][:, kvh, 512 * th:512 * th + 512], start=True, stop=True))
            W("vector", "t")
            V(lambda e, kvh=kvh, th=th: e.tensor_tensor(
                out=sb["t32"][:, 0, :], in0=sb["k_sb"][:, kvh, 512 * th:512 * th + 512],
                in1=sb["c_cosF"][:, 512 * th:512 * th + 512], op=ALU.mult))
            V(lambda e, th=th: e.tensor_tensor(out=sb["t32"][:, 1, :], in0=pb[6][0:128, :],
                                               in1=sb["c_sinF"][:, 512 * th:512 * th + 512], op=ALU.mult))
            V(lambda e, kvh=kvh, th=th: e.tensor_tensor(
                out=sb["rk_own"][:, kvh, 512 * th:512 * th + 512], in0=sb["t32"][:, 0, :],
                in1=sb["t32"][:, 1, :], op=ALU.add))
        else:
            kvh = oc - 40
            W("vector", "t")
            V(lambda e, kvh=kvh, th=th, bank=bank: e.tensor_copy(
                out=sb["vf"][:, kvh, 512 * th:512 * th + 512], in_=bank[0:128, :]))

    stream_mm_dram("qkv", 12, 32, "lnh", qkv_evac)

    if KTRUNC < 5:
        return finish(nc, ops, cnt, W, DS, sb, out_hbm, block_emit)
    W("sync", "dA")
    W("tensor", "v", cnt["v"])
    W("tensor", "s", cnt["s"])
    for ic in range(32):
        for th in range(2):
            W("sync", "t", cnt["t"])
            DS(lambda e, ic=ic, th=th: e.dma_start(out=sb["mv"][:, th, :],
                                                   in_=dr["lnh"][:, ic, 512 * th:512 * th + 512]))
        DS(lambda e, ic=ic: e.dma_start(out=sb["wst"][:, ic % 2, 0:128],
                                        in_=dr["wgl"][:, 128 * ic:128 * ic + 128], transpose=True))
        W("tensor", "dS")
        for th in range(2):
            T(lambda e, ic=ic, th=th: e.matmul(pb[6 + th][0:128, :], sb["wst"][:, ic % 2, 0:128],
                                               sb["mv"][:, th, :], start=(ic == 0), stop=(ic == 31)))
    for th in range(2):
        W("scalar", "t")
        Sc(lambda e, th=th: e.activation(out=sb["g_all"][:, 512 * th:512 * th + 512], in_=pb[6 + th][0:128, :],
                                         func=AF.Sigmoid, bias=sb["c_bgate"][:, 0:1], scale=1.0))

    if KTRUNC < 6:
        return finish(nc, ops, cnt, W, DS, sb, out_hbm, block_emit)
    W("vector", "s")
    V(lambda e: e.tensor_tensor(out=sb["kpk"][:, :, :, :],
                                in0=sb["k_sb"][:, :, :].rearrange("p h (b j) -> p h b j", j=16),
                                in1=sb["c_kpos"][:, :].rearrange("p (h j) -> p h j", j=16).unsqueeze(2).to_broadcast([128, 8, 64, 16]),
                                op=ALU.add))
    V(lambda e: e.tensor_tensor(out=sb["vpv"][:, :, :, :],
                                in0=sb["vf"][:, :, :].rearrange("p h (b j) -> p h b j", j=16),
                                in1=sb["c_vpos"][:, :].rearrange("p (h j) -> p h j", j=16).unsqueeze(2).to_broadcast([128, 8, 64, 16]),
                                op=ALU.add))

    def mlp(w1kind, inb, bias1, w2T, bias2, dst):
        for og in range(4):
            W("tensor", "v", cnt["v"])
            W("tensor", "s", cnt["s"])
            for ci in range(16):
                buf = ci % 2
                W("sync", "t", cnt["t"])
                for ap_, col0, ncols in wsrc_pieces(w1kind, og, ci):
                    DS(lambda e, ap_=ap_, col0=col0, buf=buf: e.dma_start(
                        out=sb["wst"][:, buf, col0:col0 + ap_.shape[0]], in_=ap_, transpose=True))
                W("tensor", "dS")
                for ol in range(4):
                    T(lambda e, ci=ci, ol=ol, buf=buf: e.matmul(
                        pb[ol][0:128, :], sb["wst"][:, buf, 128 * ol:128 * ol + 128],
                        sb[inb][:, :, :, ci].rearrange("p h b -> p (h b)"), start=(ci == 0), stop=(ci == 15)))
            for ol in range(4):
                hc = 4 * og + ol
                W("scalar", "t")
                Sc(lambda e, hc=hc, ol=ol, bias1=bias1: e.activation(
                    out=sb["hid"][:, hc, :], in_=pb[ol][0:128, :], func=AF.Relu,
                    bias=sb[bias1][:, hc:hc + 1], scale=1.0))
        W("tensor", "s")
        for hc in range(16):
            W("sync", "t", cnt["t"])
            DS(lambda e, hc=hc, w2T=w2T: e.dma_start(out=sb["wst"][:, hc % 2, 0:128],
                                                     in_=dr[w2T][:, 128 * hc:128 * hc + 128], transpose=True))
            W("tensor", "dS")
            T(lambda e, hc=hc: e.matmul(pb[4][0:128, :], sb["wst"][:, hc % 2, 0:128],
                                        sb["hid"][:, hc, :], start=(hc == 0), stop=(hc == 15)))
        W("vector", "t")
        dstap = (sb[dst][:, :, :].rearrange("p a b -> p (a b)") if len(sb[dst].shape) == 3
                 else sb[dst][:, :])
        V(lambda e, bias2=bias2, dstap=dstap: e.tensor_scalar(
            out=dstap, in0=pb[4][0:128, :],
            scalar1=sb[bias2][:, 0:1], scalar2=None, op0=ALU.add))

    mlp("kw1", "kpk", "c_kb1", "kw2l", "c_kb2", "ckf")
    mlp("vw1", "vpv", "c_vb1", "vw2l", "c_vb2", "pex")
    pexv = sb["pex"].rearrange("p (h b) -> p h b", b=64)
    for kvh in range(8):
        W("tensor", "v")
        T(lambda e, kvh=kvh: e.transpose(pbf[5][0:64, 0:128], pexv[:, kvh, :], sb["c_ident"][:, :]))
        W("vector", "t")
        V(lambda e, kvh=kvh: e.tensor_copy(out=sb["cvf"][0:64, kvh, :], in_=pbf[5][0:64, 0:128]))

    for kvh in range(8):
        for tc in range(8):
            bank = tc % 2
            W("tensor", "v")
            T(lambda e, kvh=kvh, tc=tc, bank=bank: e.transpose(
                pbf[bank][0:128, 0:128], sb["vf"][:, kvh, 128 * tc:128 * tc + 128], sb["c_ident"][:, :]))
            W("vector", "t")
            W("vector", "dG", cnt["dG"])
            V(lambda e, bank=bank: e.tensor_copy(out=sb["vtmp"][:, 0:128], in_=pbf[bank][0:128, 0:128]))
            W("gpsimd", "v")
            DG(lambda e, kvh=kvh, tc=tc: e.dma_start(
                out=dr["kc_in"][128 * 8192:2 * 128 * 8192].rearrange("(p c) -> p c", p=128)[:, 1024 * tc + 128 * kvh:1024 * tc + 128 * kvh + 128],
                in_=sb["vtmp"][:, 0:128]))
    W("gpsimd", "v")
    DG(lambda e: e.dma_start(out=dr["kc_in"][0:128 * 8192].rearrange("(p c) -> p c", p=128),
                             in_=sb["rk_own"][:, :, :].rearrange("p a b -> p (a b)")))
    CKB = 2 * 128 * 8192
    DG(lambda e: e.dma_start(
        out=dr["kc_in"][CKB:CKB + 128 * 512].rearrange("(p c) -> p c", p=128),
        in_=sb["ckf"][:, :, :].rearrange("p a b -> p (a b)")))
    DG(lambda e: e.dma_start(
        out=dr["kc_in"][CKB + 128 * 512:CKB + 128 * 512 + 64 * 1024].rearrange("(p c) -> p c", p=64),
        in_=sb["cvf"][0:64, :, :].rearrange("p a b -> p (a b)")))
    W("gpsimd", "dG")
    CC(lambda e: e.collective_compute("AllGather", ALU.bypass,
                                      replica_groups=[[0, 1], [2, 3], [4, 5], [6, 7]],
                                      ins=[dr["kc_in"][:]], outs=[dr["kc_out"][:, :].rearrange("a b -> (a b)")]))
    W("sync", "cc")
    for s_ in range(2):
        DS(lambda e, s_=s_: e.dma_start(out=sb["rk_sb"][:, s_, :, :].rearrange("p a b -> p (a b)"),
                                        in_=dr["kc_out"][s_, 0:128 * 8192].rearrange("(p c) -> p c", p=128)))
        DS(lambda e, s_=s_: e.dma_start(
            out=sb["v_sb"][:, s_, :, :, :].rearrange("p a b c -> p (a b c)"),
            in_=dr["kc_out"][s_, 128 * 8192:2 * 128 * 8192].rearrange("(p c) -> p c", p=128)))
        DS(lambda e, s_=s_: e.dma_start(out=sb["ck_sb"][:, :, s_, :],
                                        in_=dr["kc_out"][s_, CKB:CKB + 128 * 512].rearrange("(p h b) -> p h b", p=128, h=8)))
        DS(lambda e, s_=s_: e.dma_start(
            out=sb["cv_sb"][64 * s_:64 * s_ + 64, :, :].rearrange("p a b -> p (a b)"),
            in_=dr["kc_out"][s_, CKB + 128 * 512:CKB + 128 * 512 + 64 * 1024].rearrange("(p c) -> p c", p=64)))

    if KTRUNC < 7:
        return finish(nc, ops, cnt, W, DS, sb, out_hbm, block_emit)
    ZOC, ZO1C = sb["c_zo"], sb["c_zo1"]
    W("sync", "dA")
    for half in range(2):
        W("vector", "t", cnt["t"])
        W("vector", "dS")
        V(lambda e: e.memset(pb[4][0:128, :], 0.0))
        for kvh in range(8):
            for g in range(4):
                W("sync", "t", cnt["t"])
                DS(lambda e, kvh=kvh, g=g, half=half: e.dma_start(
                    out=sb["qg"][:, :], in_=dr["qrh"][:, 4 * kvh + g, 512 * half:512 * half + 512]))
                W("tensor", "dS")
                W("tensor", "v", cnt["v"])
                W("tensor", "s", cnt["s"])
                T(lambda e, kvh=kvh: e.matmul(pb[0][0:128, :],
                                              sb["ck_sb"][:, kvh, :, :].rearrange("p a b -> p (a b)"),
                                              sb["qg"][:, :], start=True, stop=False))
                T(lambda e, half=half: e.matmul(pb[0][0:128, :], sb["c_ident"][:, :],
                                                sb["c_cmask"][:, 512 * half:512 * half + 512],
                                                start=False, stop=True))
                T(lambda e, kvh=kvh: e.matmul(pb[5][0:1, :], sb["memkb"][:, kvh:kvh + 1],
                                              sb["qg"][:, :], start=True, stop=True))
                W("vector", "t")
                if g == 0:
                    V(lambda e: e.tensor_scalar_mul(sb["m_sb"][:, :], pb[0][0:128, :], 1.0))
                else:
                    V(lambda e: e.tensor_tensor(out=sb["m_sb"][:, :], in0=sb["m_sb"][:, :],
                                                in1=pb[0][0:128, :], op=ALU.add))
                W("scalar", "v")
                W("scalar", "t", cnt["t"])
                Sc(lambda e, g=g: e.activation(out=sb["pexpg"][:, g, :], in_=pb[0][0:128, :],
                                               func=AF.Exp, bias=sb["cB"][:, 0:1], scale=SCALE))
                Sc(lambda e, g=g: e.activation(out=sb["pexpm"][:, g, :], in_=pb[5][0:1, :],
                                               func=AF.Exp, bias=sb["cB"][0:1, 0:1], scale=SCALE))
            W("tensor", "v")
            for qc in range(4):
                T(lambda e, qc=qc: e.transpose(pb[6][0:128, 128 * qc:128 * qc + 128],
                                               sb["m_sb"][:, 128 * qc:128 * qc + 128],
                                               sb["c_identf"][:, :]))
            W("vector", "t")
            for qc in range(4):
                V(lambda e, qc=qc: e.max(out=sb["mx8"][:, 8 * qc:8 * qc + 8], in_=pb[6][0:128, 128 * qc:128 * qc + 128]))
                V(lambda e, qc=qc: e.tensor_scalar(out=sb["sm"][:, 0:1], in0=sb["mx8"][:, 8 * qc:8 * qc + 1],
                                                   scalar1=0.25, scalar2=-1000.0, op0=ALU.mult, op1=ALU.max))
                V(lambda e: e.tensor_scalar_mul(sb["sm"][:, 1:2], sb["sm"][:, 0:1], -1.0))
                W("scalar", "v")
                W("scalar", "t", cnt["t"])
                Sc(lambda e, qc=qc: e.activation(out=sb["tq"][:, 0:128], in_=pb[6][0:128, 128 * qc:128 * qc + 128],
                                                 func=AF.Exp, bias=sb["sm"][:, 1:2], scale=0.25))
                Sc(lambda e: e.activation(out=sb["sm"][:, 3:4], in_=sb["sm"][:, 1:2],
                                          func=AF.Exp, bias=sb["cB"][:, 1:2], scale=1.0))
                W("vector", "s")
                V(lambda e: e.reduce_sum(out=sb["sm"][:, 2:3], in_=sb["tq"][:, 0:128], axis=mybir.AxisListType.X))
                V(lambda e: e.tensor_tensor(out=sb["sm"][:, 2:3], in0=sb["sm"][:, 2:3],
                                            in1=sb["sm"][:, 3:4], op=ALU.add))
                W("scalar", "v")
                Sc(lambda e: e.activation(out=sb["sm"][:, 4:5], in_=sb["sm"][:, 2:3],
                                          func=AF.Ln, bias=0.0, scale=1.0))
                W("vector", "s")
                V(lambda e: e.tensor_tensor(out=sb["sm"][:, 5:6], in0=sb["sm"][:, 0:1],
                                            in1=sb["sm"][:, 4:5], op=ALU.add))
                V(lambda e: e.tensor_scalar(out=sb["sm"][:, 5:6], in0=sb["sm"][:, 5:6],
                                            scalar1=LN10, scalar2=4.0, op0=ALU.add, op1=ALU.mult))
                V(lambda e, qc=qc: e.tensor_scalar(out=sb["tq"][:, 0:128], in0=pb[6][0:128, 128 * qc:128 * qc + 128],
                                                   scalar1=sb["mx8"][:, 8 * qc + 3:8 * qc + 4], scalar2=None, op0=ALU.is_ge))
                V(lambda e, qc=qc: e.tensor_scalar(out=sb["tq"][:, 128:256], in0=pb[6][0:128, 128 * qc:128 * qc + 128],
                                                   scalar1=sb["sm"][:, 5:6], scalar2=None, op0=ALU.is_gt))
                V(lambda e, qc=qc: e.tensor_tensor(out=sb["sel01"][:, 128 * qc:128 * qc + 128],
                                                   in0=sb["tq"][:, 0:128], in1=sb["tq"][:, 128:256], op=ALU.mult))
            W("tensor", "v")
            for qc in range(4):
                T(lambda e, qc=qc: e.transpose(pbf[6][0:128, 128 * qc:128 * qc + 128],
                                               sb["sel01"][:, 128 * qc:128 * qc + 128], sb["c_ident"][:, :]))
            W("vector", "t")
            V(lambda e: e.tensor_scalar(out=sb["selneg"][:, :], in0=pbf[6][0:128, 0:512],
                                        scalar1=1e30, scalar2=-1e30, op0=ALU.mult, op1=ALU.add))

            for g in range(4):
                r0 = 16 * kvh + 3 * g
                W("sync", "t", cnt["t"])
                DS(lambda e, kvh=kvh, g=g, half=half: e.dma_start(
                    out=sb["rqg"][:, :], in_=dr["qph"][:, 4 * kvh + g, 512 * half:512 * half + 512]))
                W("tensor", "v", cnt["v"])
                W("tensor", "s", cnt["s"])
                T(lambda e, kvh=kvh, g=g: e.matmul(pb[1][0:128, :], sb["cv_sb"][:, kvh, :],
                                                   sb["pexpg"][:, g, :], start=True, stop=False))
                T(lambda e, kvh=kvh, g=g: e.matmul(pb[1][0:128, :], sb["memvb"][:, 128 * kvh:128 * kvh + 128],
                                                   sb["pexpm"][:, g, :], start=False, stop=True))
                T(lambda e, r0=r0, g=g: e.matmul(pb[4][0:r0 + 1, :], ZOC[:, 255 - r0:256],
                                                 sb["pexpg"][:, g, :], start=False, stop=False, skip_group_check=True))
                T(lambda e, r0=r0, g=g: e.matmul(pb[4][0:r0 + 1, :], ZO1C[:, 255 - r0:256],
                                                 sb["pexpm"][:, g, :], start=False, stop=False, skip_group_check=True))
                W("tensor", "dS")
                first_pv = True
                for s_ in range(2):
                    for lt in range(4 * half + 4):
                        T(lambda e, s_=s_, lt=lt, kvh=kvh: e.matmul(
                            pb[0][0:128, :], sb["rk_sb"][:, s_, kvh, 128 * lt:128 * lt + 128],
                            sb["rqg"][:, :], start=True, stop=False))
                        T(lambda e, s_=s_, lt=lt: e.matmul(
                            pb[0][0:128, :], sb["c_emask"][:, 128 * (8 * s_ + lt):128 * (8 * s_ + lt) + 128],
                            sb["selneg"][:, :], start=False, stop=True))
                        qc = lt - 4 * half
                        if 0 <= qc < 4:
                            W("vector", "t")
                            V(lambda e, s_=s_, qc=qc: e.tensor_tensor(
                                out=pb[0][0:128, 128 * qc:128 * qc + 128],
                                in0=pb[0][0:128, 128 * qc:128 * qc + 128],
                                in1=sb["c_ownm"][:, 128 * s_:128 * s_ + 128], op=ALU.add))
                            W("scalar", "v")
                        W("scalar", "t", cnt["t"])
                        Sc(lambda e: e.activation(out=sb["pex"][:, :], in_=pb[0][0:128, :],
                                                  func=AF.Exp, bias=sb["cB"][:, 0:1], scale=SCALE))
                        W("tensor", "s")
                        T(lambda e, r0=r0: e.matmul(pb[4][0:r0 + 2, :], ZOC[:, 255 - (r0 + 1):256],
                                                    sb["pex"][:, :], start=False, stop=False, skip_group_check=True))
                        T(lambda e, s_=s_, lt=lt, kvh=kvh, first_pv=first_pv: e.matmul(
                            pb[2][0:128, :], sb["v_sb"][:, s_, lt, kvh, :], sb["pex"][:, :],
                            start=first_pv, stop=False, skip_group_check=True))
                        first_pv = False
                first_s = True
                for j, (s_, rel) in enumerate(((0, -1), (0, 0), (1, -1), (1, 0))):
                    tiles = [(qc, 4 * half + qc + rel) for qc in range(4) if 0 <= 4 * half + qc + rel < 8]
                    if not tiles:
                        continue
                    for qc, ltk in tiles:
                        T(lambda e, s_=s_, ltk=ltk, kvh=kvh, qc=qc: e.matmul(
                            pb[0][0:128, 128 * qc:128 * qc + 128],
                            sb["rk_sb"][:, s_, kvh, 128 * ltk:128 * ltk + 128],
                            sb["rqg"][:, 128 * qc:128 * qc + 128], start=True, stop=True, skip_group_check=True))
                    W("vector", "t")
                    V(lambda e, j=j: e.tensor_tensor(out=pb[0][0:128, :], in0=pb[0][0:128, :],
                                                     in1=sb["c_slm"][:, 512 * j:512 * j + 512], op=ALU.add))
                    W("scalar", "v")
                    W("scalar", "t", cnt["t"])
                    Sc(lambda e: e.activation(out=sb["pex"][:, :], in_=pb[0][0:128, :],
                                              func=AF.Exp, bias=sb["cB"][:, 0:1], scale=SCALE))
                    W("tensor", "s")
                    T(lambda e, r0=r0: e.matmul(pb[4][0:r0 + 3, :], ZOC[:, 255 - (r0 + 2):256],
                                                sb["pex"][:, :], start=False, stop=False, skip_group_check=True))
                    for qc, ltk in tiles:
                        T(lambda e, s_=s_, ltk=ltk, kvh=kvh, qc=qc, first_s=first_s: e.matmul(
                            pb[3][0:128, 128 * qc:128 * qc + 128], sb["v_sb"][:, s_, ltk, kvh, :],
                            sb["pex"][:, 128 * qc:128 * qc + 128],
                            start=first_s, stop=False, skip_group_check=True))
                    first_s = False
                W("vector", "t")
                V(lambda e: e.tensor_copy(out=sb["zs"][:, :], in_=pb[4][0:128, :]))
                W("tensor", "v")
                T(lambda e, r0=r0: e.matmul(pb[5][0:3, :], sb["c_identf"][:, r0:r0 + 3],
                                            sb["zs"][:, :], start=True, stop=True))
                T(lambda e, r0=r0, half=half: e.matmul(pb[6][0:3, :], sb["c_ident"][:, r0:r0 + 3],
                                                       sb["g_all"][:, 512 * half:512 * half + 512],
                                                       start=True, stop=True))
                W("vector", "t")
                V(lambda e: e.tensor_scalar(out=sb["scv"][0:3, :], in0=pb[5][0:3, :],
                                            scalar1=1e-30, scalar2=None, op0=ALU.add))
                V(lambda e: e.reciprocal(out=sb["scv"][0:3, :], in_=sb["scv"][0:3, :]))
                V(lambda e: e.tensor_tensor(out=sb["scv"][0:3, :], in0=sb["scv"][0:3, :],
                                            in1=pb[6][0:3, :], op=ALU.mult))
                V(lambda e: e.tensor_copy(out=sb["sclk"][0:3, :], in_=sb["scv"][0:3, :]))
                W("tensor", "v")
                for br in range(3):
                    T(lambda e, br=br: e.matmul(pb[5][0:128, :],
                                                sb["c_b3v"][0:3, 128 * br:128 * br + 128],
                                                sb["sclk"][0:3, :], start=True, stop=True, skip_group_check=True))
                    W("vector", "t")
                    bsrc = [pb[1], pb[2], pb[3]][br]
                    V(lambda e: e.tensor_copy(out=sb["m_sb"][:, :], in_=pb[5][0:128, :]))
                    if br == 0:
                        V(lambda e, bsrc=bsrc: e.tensor_tensor(out=sb["g1"], in0=bsrc[0:128, :],
                                                               in1=sb["m_sb"][:, :], op=ALU.mult))
                    elif br == 1:
                        V(lambda e, bsrc=bsrc: e.tensor_tensor(out=sb["g2"], in0=bsrc[0:128, :],
                                                               in1=sb["m_sb"][:, :], op=ALU.mult))
                        V(lambda e: e.tensor_tensor(out=sb["g1"], in0=sb["g1"],
                                                    in1=sb["g2"], op=ALU.add))
                    else:
                        V(lambda e, bsrc=bsrc: e.tensor_tensor(out=sb["g2"], in0=bsrc[0:128, :],
                                                               in1=sb["m_sb"][:, :], op=ALU.mult))
                        W("vector", "dA", cnt["dA"])
                        V(lambda e: e.tensor_tensor(out=sb["ao"][:, :], in0=sb["g1"],
                                                    in1=sb["g2"], op=ALU.add))
                    W("tensor", "v")
                W("scalar", "v")
                DA(lambda e, kvh=kvh, g=g, half=half: e.dma_start(
                    out=dr["ath"][:, 4 * kvh + g, 512 * half:512 * half + 512], in_=sb["ao"][:, :]))

    if KTRUNC < 8:
        return finish(nc, ops, cnt, W, DS, sb, out_hbm, block_emit)
    def wo_evac(oc, th, bank):
        W("scalar", "v", cnt["v"])
        DA(lambda e, oc=oc, th=th: e.dma_start(out=sb["xsq"][:, :],
                                               in_=dr["x1h"][:, oc, 512 * th:512 * th + 512]))
        W("vector", "t")
        W("vector", "dA")
        V(lambda e, bank=bank: e.tensor_copy(out=sb["t32"][:, 0, :], in_=bank[0:128, :]))
        V(lambda e: e.tensor_tensor(out=sb["vtmp"][:, :], in0=sb["t32"][:, 0, :],
                                    in1=sb["xsq"][:, :], op=ALU.add))
        W("scalar", "v")
        DA(lambda e, oc=oc, th=th: e.dma_start(out=dr["x2h"][:, oc, 512 * th:512 * th + 512],
                                               in_=sb["vtmp"][:, :]))

    stream_mm_dram("wo", 8, 32, "ath", wo_evac)

    if KTRUNC < 9:
        return finish(nc, ops, cnt, W, DS, sb, out_hbm, block_emit)
    layernorm("x2h", "lnh")
    stream_mm_dram("ffn", 8, 32, "lnh", ffn_evac("x2h", None, final=True))

    return finish(nc, ops, cnt, W, DS, sb, out_hbm, block_emit, full=True)


def block_emit(nc, ops):
    with nc.Block() as block:
        semh = {}
        for s_ in ("t", "v", "s", "dS", "dT", "dG", "dA", "cc"):
            semh[s_] = nc.alloc_semaphore("sem_" + s_)

        def emit(engine):
            def body(eng):
                for item in ops[engine]:
                    if item[0] == "w":
                        eng.wait_ge(semh[item[1]], item[2])
                    else:
                        _, fn, inc, by = item
                        ins = fn(eng)
                        if inc is not None and ins is not None:
                            ins.then_inc(semh[inc], by)
            return body

        block.tensor(emit("tensor"))
        block.vector(emit("vector"))
        block.scalar(emit("scalar"))
        block.gpsimd(emit("gpsimd"))
        block.sync(emit("sync"))
    return nc


def finish(nc, ops, cnt, W, DS, sb, out_hbm, emitter, full=False):
    if not full:
        for oc in range(32):
            DS(lambda e, oc=oc: e.dma_start(out=out_hbm[:, oc, :],
                                            in_=sb["zs"][:, :].to_broadcast([128, 1024]) if False else sb["AR1"][:, 0:1024].bitcast(F32)[:, 0:512].to_broadcast([128, 1024]) if False else sb["mubc"][:, :].to_broadcast([128, 1024]) if False else sb["AR1"].bitcast(F32)[:, 0:1024]))
    W("sync", "dA")
    W("sync", "dS")
    W("sync", "dG")
    W("sync", "cc")
    return emitter(nc, ops)


def _consts(par):
    c = {}
    c["ident"] = np.eye(128, dtype=np.float32).astype(BF)
    c["identf"] = np.eye(128, dtype=np.float32)
    c["ones1f"] = np.ones((1, 128), np.float32)
    swp = np.zeros((128, 128), np.float32)
    for f in range(64):
        swp[2 * f + 1, 2 * f] = 1.0
        swp[2 * f, 2 * f + 1] = 1.0
    c["swp"] = swp.astype(BF)
    st = np.zeros((128, 4), np.float32)
    st[:, 0] = 1.0
    st[:, 3] = 1.0
    c["stat2"] = st.astype(BF)
    c["ones1"] = np.ones((1, 128), np.float32).astype(BF)
    zo = np.zeros((128, 256), np.float32)
    zo[:, 255] = 0.0
    zo[:, 255] = 1.0
    c["zo"] = zo.astype(BF)
    zo1 = np.zeros((1, 256), np.float32)
    zo1[:, 255] = 1.0
    c["zo1"] = zo1.astype(BF)
    b3 = np.zeros((3, 384), np.float32)
    for br in range(3):
        b3[br, 128 * br:128 * br + 128] = 1.0
    c["b3v"] = b3.astype(BF)
    em = np.zeros((128, 2048), np.float32)
    for v in range(16):
        for k in range(128):
            em[v * 8 + k // 16, v * 128 + k] = 1.0
    c["emask"] = em.astype(BF)
    cm = np.zeros((128, TOK), np.float32)
    qpos = np.zeros(TOK, np.int64)
    for lc in range(8):
        qpos[128 * lc:128 * lc + 128] = (2 * lc + par) * 128 + np.arange(128)
    for s in range(2):
        for lb in range(64):
            sqb = (2 * (lb // 8) + s) * 8 + lb % 8
            cm[s * 64 + lb, :] = np.where(sqb * 16 + 15 < qpos, 0.0, -1e30)
    c["cmask"] = cm.astype(BF)
    own = np.zeros((128, 256), np.float32)
    pat = np.zeros((128, 128), np.float32)
    for k in range(128):
        for q in range(128):
            if k // 16 == q // 16 and k % 16 <= q % 16:
                pat[k, q] = 1e30
    own[:, 128 * par:128 * par + 128] = pat
    c["ownm"] = own.astype(BF)
    bandB = np.zeros((128, 128), np.float32)
    bandA = np.zeros((128, 128), np.float32)
    for k in range(128):
        for q in range(128):
            bandB[k, q] = 0.0 if (q - 64 <= k <= q) else -1e30
            bandA[k, q] = 0.0 if (k >= q + 64) else -1e30
    negm = np.full((128, 128), -1e30, np.float32)
    if par == 0:
        mdict = {(0, -1): negm, (0, 0): bandB, (1, -1): bandA, (1, 0): negm}
    else:
        mdict = {(0, -1): negm, (0, 0): bandA, (1, -1): negm, (1, 0): bandB}
    slm = np.zeros((128, 2048), np.float32)
    for j, key in enumerate(((0, -1), (0, 0), (1, -1), (1, 0))):
        slm[:, 512 * j:512 * j + 512] = np.tile(mdict[key], (1, 4))
    c["slm"] = slm.astype(BF)
    inv = 1.0 / 10000.0 ** (np.arange(0, DH, 2, dtype=np.float64) / DH)
    ang = qpos[None, :].astype(np.float64) * inv[:, None]
    cosF = np.zeros((128, TOK), np.float64)
    sinF = np.zeros((128, TOK), np.float64)
    cosF[0::2, :] = np.cos(ang)
    cosF[1::2, :] = np.cos(ang)
    sinF[0::2, :] = -np.sin(ang)
    sinF[1::2, :] = np.sin(ang)
    c["cosF"] = cosF.astype(np.float32)
    c["sinF"] = sinF.astype(np.float32)
    return c


def kernel(L0_attn_hidden_states, L0_attn_w_qkv, L0_attn_w_o, L0_attn_k_pos,
           L0_attn_v_pos, L0_attn_k_w1, L0_attn_k_b1, L0_attn_k_w2, L0_attn_k_b2,
           L0_attn_v_w1, L0_attn_v_b1, L0_attn_v_w2, L0_attn_v_b2, L0_attn_mem_kv,
           L0_attn_w_gate, L0_attn_b_gate, L0_ffn_q_scales, L0_ffn_q_vals, L0_ffn_bias):
    hs = np.ascontiguousarray(L0_attn_hidden_states)
    nc = build_nc()
    in_maps = []
    for core in range(8):
        b, par = core // 2, core % 2
        idx = np.concatenate([np.arange(128) + (2 * lc + par) * 128 for lc in range(8)])
        m = {}
        xf = hs[b][idx].T
        m["x_in"] = np.ascontiguousarray(xf.reshape(32, 128, TOK).transpose(1, 0, 2))
        qvs = np.ascontiguousarray(L0_ffn_q_vals[512 * core:512 * core + 512])
        m["qv"] = np.ascontiguousarray(qvs.reshape(4, 128, 4096).transpose(1, 0, 2))
        m["qs"] = np.ascontiguousarray(L0_ffn_q_scales[512 * core:512 * core + 512].reshape(4, 128, 128).transpose(1, 0, 2))
        m["wqkv_s"] = np.ascontiguousarray(L0_attn_w_qkv[768 * core:768 * core + 768])
        m["wo_s"] = np.ascontiguousarray(L0_attn_w_o[512 * core:512 * core + 512])
        m["kw1_s"] = np.ascontiguousarray(L0_attn_k_w1[256 * core:256 * core + 256])
        m["vw1_s"] = np.ascontiguousarray(L0_attn_v_w1[256 * core:256 * core + 256])
        wg = np.zeros((128, D), np.float32)
        bg = np.zeros((128, 1), np.float32)
        for kvh in range(KVH):
            for g in range(G):
                for j in range(3):
                    wg[16 * kvh + 3 * g + j] = L0_attn_w_gate[3 * (4 * kvh + g) + j]
                    bg[16 * kvh + 3 * g + j, 0] = L0_attn_b_gate[3 * (4 * kvh + g) + j]
        m["wg"], m["bgate"] = wg, bg
        m["kw2"] = np.ascontiguousarray(L0_attn_k_w2)
        m["vw2"] = np.ascontiguousarray(L0_attn_v_w2)
        m["kb1"] = np.ascontiguousarray(L0_attn_k_b1.reshape(16, 128).T)
        m["vb1"] = np.ascontiguousarray(L0_attn_v_b1.reshape(16, 128).T)
        m["kb2"] = np.ascontiguousarray(L0_attn_k_b2.reshape(128, 1))
        m["vb2"] = np.ascontiguousarray(L0_attn_v_b2.reshape(128, 1))
        m["ffnb"] = np.ascontiguousarray(L0_ffn_bias.reshape(32, 128).T)
        m["kpos"] = np.ascontiguousarray(L0_attn_k_pos.transpose(2, 0, 1).reshape(128, 128))
        m["vpos"] = np.ascontiguousarray(L0_attn_v_pos.transpose(2, 0, 1).reshape(128, 128))
        m["memk"] = np.ascontiguousarray(L0_attn_mem_kv[0, :, 0, :].T)
        m["memv"] = np.ascontiguousarray(L0_attn_mem_kv[1, :, 0, :].reshape(1, 1024))
        cc = _consts(par)
        for k_, v_ in cc.items():
            m[k_] = v_
        in_maps.append(m)
    res = run_bass_kernel_spmd(nc, in_maps, core_ids=list(range(8)))
    out = np.zeros((4, 2048, 4096), np.float32)
    for core in range(8):
        b, par = core // 2, core % 2
        idx = np.concatenate([np.arange(128) + (2 * lc + par) * 128 for lc in range(8)])
        o = res.results[core]["out"]
        of = o.transpose(1, 0, 2).reshape(4096, TOK)
        out[b][idx] = of.T
    return out
